# version 07
# baseline (speedup 1.0000x reference)
import numpy as np
import ml_dtypes
import concourse.bass as bass
import concourse.mybir as mybir
from concourse.bass_utils import run_bass_kernel_spmd

F32 = mybir.dt.float32
BF16 = mybir.dt.bfloat16
U8 = mybir.dt.uint8
AF = mybir.ActivationFunctionType
ALU = mybir.AluOpType
AX = mybir.AxisListType

D = 2048
S = 2048
NH, NKV, HD = 32, 8, 64
FF = 7168
NE, RK = 8, 32
ER = NE * RK
MF = FF // 128
KH = D // 128
NT = S // 128
NS = 4
EPS = 1e-5
G_A2 = 3
G_DN = 8

ENGINES = ("tensor", "vector", "scalar", "gpsimd", "sync")


def _lhsT_stream(W):
    Kd, Md = W.shape
    return np.ascontiguousarray(
        W.reshape(Kd // 128, 128, Md // 128, 128).transpose(2, 1, 0, 3))


def _rope_tables():
    inv_freq = 1.0 / (10000.0 ** (np.arange(0, HD, 2, dtype=np.float32) / HD))
    pos = np.arange(S, dtype=np.float32)
    freqs = pos[:, None] * inv_freq[None, :]
    emb = np.concatenate([freqs, freqs], axis=-1)
    cos = np.cos(emb).astype(np.float32)
    sin = np.sin(emb).astype(np.float32)
    sins = sin.copy()
    sins[:, :HD // 2] = -sins[:, :HD // 2]
    cos_f = np.ascontiguousarray(cos.reshape(NT, 128, HD))
    sins_f = np.ascontiguousarray(sins.reshape(NT, 128, HD))
    return (cos_f.astype(ml_dtypes.bfloat16), sins_f.astype(ml_dtypes.bfloat16),
            cos_f, sins_f)


class MK:
    def __init__(self, dev_outputs=False, detect_races=False):
        self.dev = dev_outputs
        self.nc = bass.Bass("TRN2", target_bir_lowering=False,
                            detect_race_conditions=detect_races)
        self.streams = {e: [] for e in ENGINES}
        self.sems = {}
        self.cnt = {}
        self.guar = {e: {} for e in ENGINES}
        self.bk_writer = {}
        self.bk_readers = {}
        for s in ("ld2", "ldz0", "ldz1", "ldz2", "ldz3", "wa0", "wa1",
                  "wu0", "wu1", "w2s", "rng", "msc", "ldx0", "ldx1",
                  "stx0", "stx1", "stc0", "stc1", "pe", "act", "dve", "plr", "plz"):
            self.sems[s] = self.nc.alloc_semaphore(s)
            self.cnt[s] = 0
        self._dram()
        self._consts()
        self._sbuf()

    def emit(self, eng, fn):
        self.streams[eng].append(fn)

    def wait(self, eng, sname, val):
        if val is None or val <= 0:
            return
        if self.guar[eng].get(sname, 0) >= val:
            return
        self.guar[eng][sname] = val
        h = self.sems[sname]
        self.emit(eng, lambda e, h=h, v=val: e.wait_ge(h, v))

    def op(self, eng, build, inc=None):
        ev = None
        pend = None
        if inc is not None:
            sname, by = inc
            self.cnt[sname] += by
            ev = self.cnt[sname]
            pend = (self.sems[sname], by)

        def fn(e, build=build, pend=pend):
            ins = build(e)
            if pend is not None:
                ins.then_inc(pend[0], pend[1])
        self.emit(eng, fn)
        return ev

    def fence_v(self):
        ev = self.op("vector", lambda e: e.engine_nop(), inc=("dve", 1))
        self.wait("vector", "dve", ev)

    def gate(self, eng, *sems):
        for s in sems:
            self.wait(eng, s, self.cnt[s])

    def bank_w(self, eng, banks):
        for b in banks:
            for sname, v in self.bk_readers.get(b, {}).items():
                self.wait(eng, sname, v)

    def bank_w_done(self, banks, sname, val):
        for b in banks:
            self.bk_writer[b] = (sname, val)
            self.bk_readers[b] = {}

    def bank_r(self, eng, banks):
        for b in banks:
            w = self.bk_writer.get(b)
            if w is not None:
                self.wait(eng, w[0], w[1])

    def bank_r_done(self, banks, sname, val):
        for b in banks:
            d = self.bk_readers.setdefault(b, {})
            d[sname] = max(d.get(sname, 0), val)

    def _dram(self):
        nc = self.nc
        self.din = {}
        self.din["x"] = nc.dram_tensor("x", [D, S], F32, kind="ExternalInput").ap()
        for nm, shp in [
            ("wq_s", [KH, 128, KH, 128]), ("wk_s", [4, 128, KH, 128]),
            ("wv_s", [4, 128, KH, 128]), ("wphi_s", [KH, 128, KH, 128]),
            ("wo_s", [KH, 128, KH, 128]),
            ("w1_s", [MF, 128, KH, 128]), ("w3_s", [MF, 128, KH, 128]),
            ("w2t", [FF, D]),
            ("b1s_s", [MF, 128, 2, 128]), ("b3s_s", [MF, 128, 2, 128]),
            ("a1st", [2, 128, KH, 128]), ("a3st", [2, 128, KH, 128]),
            ("a2st", [FF, ER]), ("b2s", [ER, D]), ("wg_t", [D, NE]),
        ]:
            self.din[nm] = nc.dram_tensor(nm, shp, F32, kind="ExternalInput").ap()
        self.y = nc.dram_tensor("y", [D, S], F32, kind="ExternalOutput").ap()
        rkind = "ExternalOutput" if self.dev else "Internal"
        self.xr = [nc.dram_tensor(f"xr{i}", [D, S], F32, kind=rkind).ap()
                   for i in (1, 2, 3)]
        self.x2c_s = nc.dram_tensor("x2c_s", [MF, 128, S], BF16, kind="Internal").ap()

    def _consts(self):
        nc = self.nc
        cos_t, sins_t, cos_f, sins_f = _rope_tables()
        self.c_identb = nc.inline_tensor(np.eye(128, dtype=ml_dtypes.bfloat16), "c_identb").ap()
        self.c_identf = nc.inline_tensor(np.eye(128, dtype=np.float32), "c_identf").ap()
        self.c_ones128 = nc.inline_tensor(np.ones((128, 1), dtype=ml_dtypes.bfloat16), "c_ones128").ap()
        self.c_ones1x = nc.inline_tensor(np.ones((1, 128), dtype=np.float32), "c_ones1x").ap()
        s8 = np.zeros((NE, ER), dtype=ml_dtypes.bfloat16)
        for e in range(NE):
            s8[e, e * RK:(e + 1) * RK] = 1.0
        self.c_s8 = nc.inline_tensor(s8, "c_s8").ap()
        self.c_cos = nc.inline_tensor(cos_t, "c_cos").ap()
        self.c_sins = nc.inline_tensor(sins_t, "c_sins").ap()
        self.c_cosF = nc.inline_tensor(cos_f, "c_cosF").ap()
        self.c_sinsF = nc.inline_tensor(sins_f, "c_sinsF").ap()

    def _sbuf(self):
        nc = self.nc
        self.A_u8 = nc.alloc_sbuf_tensor("A_u8", [128, 65536], U8)
        self.B_u8 = nc.alloc_sbuf_tensor("B_u8", [128, 65536], U8)
        self.C_u8 = nc.alloc_sbuf_tensor("C_u8", [128, 73728], U8)
        self.D_u8 = nc.alloc_sbuf_tensor("D_u8", [128, 6656], U8)
        self.zA = self.cv(self.A_u8, 0, 32768, BF16)
        self.accA = self.cv(self.A_u8, 0, 16384, F32)
        self.Bb = self.cv(self.B_u8, 0, 32768, BF16)
        d = self.D_u8
        self.identb = self.cv(d, 0, 128, BF16)
        self.identf = self.cv(d, 256, 128, F32)
        self.ones128 = self.cv(d, 768, 1, BF16)
        self.ones1x = self.cv(d, 800, 128, F32, rows=1)
        self.s8 = self.cv(d, 1312, ER, BF16, rows=NE)
        self.cosD = self.cv(d, 1824, NT * HD, BF16)
        self.sinsD = self.cv(d, 3872, NT * HD, BF16)
        self.PS = [nc.alloc_psum_tensor("psA", [128, 2048], F32),
                   nc.alloc_psum_tensor("psB", [128, 2048], F32)]

    @staticmethod
    def cv(arena, byte_off, cols, dtype, rows=128):
        esz = 4 if dtype == F32 else (2 if dtype == BF16 else 1)
        ap = arena[:, byte_off:byte_off + cols * esz].bitcast(dtype)
        if rows != 128:
            ap = ap[0:rows, :]
        return ap

    def ps(self, t, s):
        return self.PS[t][:, s * 512:(s + 1) * 512]

    def psb(self, t, s2):
        return self.PS[t][:].bitcast(BF16)[:, s2 * 1024:(s2 + 1) * 1024]

    def cast_dma(self, dst, src, sname):
        return self.op("gpsimd", lambda e, d=dst, s=src: e.dma_start(out=d, in_=s),
                       inc=(sname, 16))

    def sync_dma(self, dst, src, sname):
        return self.op("sync", lambda e, d=dst, s=src: e.dma_start(out=d, in_=s),
                       inc=(sname, 16))

    def wait_zall(self, eng):
        for sname, v in self._z_final:
            self.wait(eng, sname, v)

    def gate_stores(self, eng):
        for s in ("stx0", "stx1", "stc0", "stc1"):
            self.wait(eng, s, self.cnt[s])

    def load_consts(self):
        ev = None
        for dst, src in [(self.identb, self.c_identb), (self.identf, self.c_identf),
                         (self.ones128, self.c_ones128), (self.ones1x, self.c_ones1x),
                         (self.s8, self.c_s8)]:
            ev = self.sync_dma(dst, src, "ld2")
        ev = self.sync_dma(self.cosD[:].rearrange("p (t d) -> p t d", t=NT),
                           self.c_cos.transpose([1, 0, 2]), "ld2")
        ev = self.sync_dma(self.sinsD[:].rearrange("p (t d) -> p t d", t=NT),
                           self.c_sins.transpose([1, 0, 2]), "ld2")
        for eng in ENGINES:
            self.wait(eng, "ld2", ev)

    def attn_pre(self, pg):
        C = self.C_u8
        wbuf = [self.cv(C, i * 4096, 2048, BF16) for i in range(2)]
        self.gate_vals("gpsimd", pg)
        self._attn_pre = [self.cast_dma(
            wbuf[m % 2][:], self.din["wq_s"][m].rearrange("p k j -> p (k j)"),
            f"wa{m % 2}") for m in range(2)]

    def ffn_pre(self, pg):
        C = self.C_u8
        din = self.din
        w1b = [self.cv(C, i * 4096, 2048, BF16) for i in range(2)]
        w3b = [self.cv(C, 8192 + i * 4096, 2048, BF16) for i in range(2)]
        b1b = [self.cv(C, 16384 + i * 512, 256, BF16) for i in range(2)]
        b3b = [self.cv(C, 17408 + i * 512, 256, BF16) for i in range(2)]
        a2b = [self.cv(C, 18432 + i * 512, 256, BF16) for i in range(G_A2)]
        wgb = self.cv(C, 19968, 128, BF16)
        self.gate_vals("gpsimd", pg)
        ev_wg = self.cast_dma(wgb[:].rearrange("p (k c) -> p k c", k=KH),
                              din["wg_t"].rearrange("(k p) c -> p k c", p=128), "msc")
        evs = []
        for m in range(2):
            par = m
            self.cast_dma(w1b[par][:],
                          din["w1_s"][m].rearrange("p k j -> p (k j)"), f"wu{par}")
            self.cast_dma(w3b[par][:],
                          din["w3_s"][m].rearrange("p k j -> p (k j)"), f"wu{par}")
            self.cast_dma(b1b[par][:].rearrange("p (k j) -> p k j", k=2),
                          din["b1s_s"][m], f"wu{par}")
            self.cast_dma(b3b[par][:].rearrange("p (k j) -> p k j", k=2),
                          din["b3s_s"][m], f"wu{par}")
            evs.append(self.cast_dma(a2b[m][:],
                                     din["a2st"][m * 128:(m + 1) * 128, :], f"wu{par}"))
        self._ffn_pre = (ev_wg, evs)

    def emit_ln(self, x_src, st_gates=None, prefetch=None):
        C = self.C_u8
        xsq = [self.cv(C, 20480 + i * 4096, 2048, BF16) for i in range(2)]
        row1 = self.cv(C, 28672, 2048, F32, rows=1)
        row2 = self.cv(C, 36864, 2048, F32, rows=1)
        row3 = self.cv(C, 45056, 2048, F32, rows=1)
        Rbf = self.cv(C, 53248, 2048, F32)
        MRb = self.cv(C, 61440, 2048, F32)
        ztmp = self.cv(C, 28672, 2048, F32)
        zA = self.zA
        if st_gates is None:
            self.gate("gpsimd", "pe", "dve", "act")
            self.gate_stores("gpsimd")
        self.gate("scalar", "pe", "dve")
        self.gate_stores("scalar")
        self.gate("vector", "pe", "act")
        self.gate_stores("vector")
        ev_ld = []
        for k in range(KH):
            if st_gates is not None:
                for sname, v in st_gates[k]:
                    self.wait("gpsimd", sname, v)
            ev = self.cast_dma(zA[:, k * S:(k + 1) * S],
                               x_src[k * 128:(k + 1) * 128, :], f"ldz{k % 4}")
            ev_ld.append((f"ldz{k % 4}", ev))
        allb = [(t, s) for t in range(2) for s in range(NS)]
        self.bank_w("tensor", allb)
        ev_sq = [None] * KH
        pe_ev = [None] * KH
        for k in range(KH):
            self.wait("scalar", ev_ld[k][0], ev_ld[k][1])
            if k >= 2:
                self.wait("scalar", "pe", pe_ev[k - 2])
            ev_sq[k] = self.op("scalar", lambda e, k=k: e.activation(
                xsq[k % 2][:], zA[:, k * S:(k + 1) * S], AF.Square), inc=("act", 1))
            self.wait("tensor", ev_ld[k][0], ev_ld[k][1])
            for s in range(NS):
                self.op("tensor", lambda e, k=k, s=s: e.matmul(
                    self.ps(0, s)[0:1, :], self.ones128[:],
                    zA[:, k * S + s * 512: k * S + (s + 1) * 512],
                    start=(k == 0), stop=(k == KH - 1)))
            self.wait("tensor", "act", ev_sq[k])
            for s in range(NS):
                pe_ev[k] = self.op("tensor", lambda e, k=k, s=s: e.matmul(
                    self.ps(1, s)[0:1, :], self.ones128[:],
                    xsq[k % 2][:, s * 512:(s + 1) * 512],
                    start=(k == 0), stop=(k == KH - 1)),
                    inc=("pe", 1) if s == NS - 1 else None)
        self.bank_w_done(allb, "pe", pe_ev[KH - 1])
        self.bank_r("vector", allb)
        for s in range(NS):
            self.op("vector", lambda e, s=s: e.tensor_copy(
                row1[:, s * 512:(s + 1) * 512], self.ps(0, s)[0:1, :]))
            self.op("vector", lambda e, s=s: e.tensor_copy(
                row2[:, s * 512:(s + 1) * 512], self.ps(1, s)[0:1, :]))
        ev = self.op("vector", lambda e: e.tensor_scalar_mul(row1[:], row1[:], 1.0 / D),
                     inc=("dve", 1))
        self.bank_r_done(allb, "dve", ev)
        self.wait("vector", "dve", ev)
        self.op("vector", lambda e: e.tensor_tensor(row3[:], row1[:], row1[:], ALU.mult))
        self.fence_v()
        ev_var = self.op("vector", lambda e: e.scalar_tensor_tensor(
            row2[:], row2[:], 1.0 / D, row3[:], ALU.mult, ALU.subtract), inc=("dve", 1))
        self.wait("vector", "dve", ev_var)
        ev_var = self.op("vector", lambda e: e.tensor_scalar_add(row2[:], row2[:], EPS),
                         inc=("dve", 1))
        self.wait("scalar", "dve", ev_var)
        self.op("scalar", lambda e: e.activation(row3[:], row2[:], AF.Ln))
        ev_rstd = self.op("scalar", lambda e: e.activation(
            row2[:], row3[:], AF.Exp, scale=-0.5), inc=("act", 1))
        self.wait("vector", "act", ev_rstd)
        ev_mr = self.op("vector", lambda e: e.tensor_tensor(
            row1[:], row1[:], row2[:], ALU.mult), inc=("dve", 1))
        self.wait("vector", "dve", ev_mr)
        self.wait("tensor", "act", ev_rstd)
        self.wait("tensor", "dve", ev_mr)
        self.bank_w("tensor", allb)
        pe_b = None
        for s in range(NS):
            self.op("tensor", lambda e, s=s: e.matmul(
                self.ps(0, s), self.ones1x[:], row2[:, s * 512:(s + 1) * 512],
                start=True, stop=True))
            pe_b = self.op("tensor", lambda e, s=s: e.matmul(
                self.ps(1, s), self.ones1x[:], row1[:, s * 512:(s + 1) * 512],
                start=True, stop=True), inc=("pe", 1))
        self.bank_w_done(allb, "pe", pe_b)
        self.bank_r("scalar", allb)
        self.op("scalar", lambda e: e.copy(Rbf[:], self.PS[0][:]))
        ev_e = self.op("scalar", lambda e: e.copy(MRb[:], self.PS[1][:]), inc=("act", 1))
        self.bank_r_done(allb, "act", ev_e)
        self.wait("vector", "act", ev_e)
        if prefetch is not None:
            prefetch()
        self.wait("gpsimd", "act", ev_e)
        ztmp2 = self.cv(C, 36864, 2048, F32)
        N_DVE = 11
        z_evs = []
        ev_z = None
        ev_zp = None
        for k in range(KH):
            if k < N_DVE:
                self.op("vector", lambda e, k=k: e.tensor_tensor(
                    ztmp[:], zA[:, k * S:(k + 1) * S], Rbf[:], ALU.mult))
                ev_z = self.op("vector", lambda e, k=k: e.tensor_tensor(
                    zA[:, k * S:(k + 1) * S], ztmp[:], MRb[:], ALU.subtract),
                    inc=("dve", 1))
                z_evs.append(("dve", ev_z))
            else:
                self.op("gpsimd", lambda e, k=k: e.tensor_tensor(
                    ztmp2[:], zA[:, k * S:(k + 1) * S], Rbf[:], ALU.mult))
                ev_zp = self.op("gpsimd", lambda e, k=k: e.tensor_tensor(
                    zA[:, k * S:(k + 1) * S], ztmp2[:], MRb[:], ALU.subtract),
                    inc=("plz", 1))
                z_evs.append(("plz", ev_zp))
        self.z_evs = z_evs
        self._z_final = [("dve", ev_z), ("plz", ev_zp)]
        return ev_z

    def emit_attn_a(self, x_src, x_dst, ev_z, stop=None, pg=None):
        C = self.C_u8
        din = self.din
        zA, Bb = self.zA, self.Bb
        wbuf = [self.cv(C, i * 4096, 2048, BF16) for i in range(2)]
        qfm = [self.cv(C, 8192 + i * 4096, 2048, BF16) for i in range(2)]
        qtm = [self.cv(C, 16384 + i * 4096, 2048, BF16) for i in range(2)]
        qkp = [self.cv(C, 24576 + i * 4096, 2048, BF16) for i in range(2)]
        kk_tm = self.cv(C, 49152, 8192, BF16)
        kfmA32 = self.cv(C, 8192, 2048, F32)
        ktm32 = self.cv(C, 16384, 2048, F32)
        kap32 = self.cv(C, 24576, 2048, F32)
        ex4 = self.cv(C, 33792, 2048, BF16, rows=4)
        cosF = self.cv(C, 37888, 1024, F32)
        sinsF = self.cv(C, 41984, 1024, F32)
        vtmA = self.cv(C, 24576, 4096, BF16)
        vtmB = self.cv(C, 65536, 4096, BF16)
        qg_sb = self.cv(self.D_u8, 5920, 16, F32)
        qgl = self.cv(self.D_u8, 5984, 32, F32)
        al_tm = self.cv(C, 32768, 512, BF16)
        nmx4 = self.cv(self.D_u8, 6112, 1, F32, rows=4)
        ssum4 = self.cv(self.D_u8, 6128, 1, F32, rows=4)
        lg_sb = self.cv(C, 0, 2048, F32, rows=4)
        ak_sb = [self.cv(C, 8192 + i * 4096, 2048, BF16) for i in range(2)]
        outer_sb = self.cv(C, 16384, 1024, BF16)
        res_sb = [self.cv(C, 24576 + i * 4096, 2048, BF16) for i in range(2)]
        xt_sb = [self.cv(C, 32768 + i * 8192, 2048, F32) for i in range(2)]
        xo_sb = [self.cv(C, 49152 + i * 8192, 2048, F32) for i in range(2)]

        def vt_ap(t, col0, ncol):
            base = vtmA if t < 8 else vtmB
            tt = t % 8
            return base[:, tt * 512 + col0: tt * 512 + col0 + ncol]

        def proj(wkey, Mb, tm_handler, fm_dst, need_kappa, first=False):
            pre = self._attn_pre if (first and getattr(self, "_attn_pre", None)) \
                else None
            if pre is not None:
                pass
            elif first and pg is not None:
                self.gate_vals("gpsimd", pg)
            else:
                self.gate("gpsimd", "pe")
            pe_m = [None] * Mb
            act_ev = [None] * Mb
            tr1_pe = [None] * Mb
            tr2_pe = [None] * Mb
            stage1_ev = [None] * Mb
            ex_ev = [None] * Mb
            projb = [(0, s) for s in range(NS)]
            tr1b = [(1, 0), (1, 1)]
            tr2b = [(1, 2), (1, 3)]
            evw = [None] * Mb
            for it in range(Mb + 2):
                m = it
                if m < Mb:
                    par = m % 2
                    if pre is not None and m < 2:
                        evw[m] = pre[m]
                        if m == 1:
                            self._attn_pre = None
                    else:
                        if m >= 2:
                            self.wait("gpsimd", "pe", pe_m[m - 2])
                        evw[m] = self.cast_dma(
                            wbuf[par][:], din[wkey][m].rearrange("p k j -> p (k j)"),
                            f"wa{par}")
                    self.wait("tensor", f"wa{par}", evw[m])
                    self.bank_w("tensor", projb)
                    lastmm = None
                    for s in range(NS):
                        for k in range(KH):
                            self.wait("tensor", *self.z_evs[k])
                            lastmm = self.op("tensor", lambda e, s=s, k=k, par=par: e.matmul(
                                self.ps(0, s), wbuf[par][:, k * 128:(k + 1) * 128],
                                zA[:, k * S + s * 512: k * S + (s + 1) * 512],
                                start=(k == 0), stop=(k == KH - 1)),
                                inc=("pe", 1) if (s == NS - 1 and k == KH - 1) else None)
                    pe_m[m] = lastmm
                    self.bank_w_done(projb, "pe", pe_m[m])
                    self.bank_r("scalar", projb)
                    if m >= 2:
                        self.wait("scalar", "pe", tr1_pe[m - 2])
                    act_ev[m] = self.op("scalar", lambda e, m=m: e.copy(
                        qfm[m % 2][:], self.PS[0][:]), inc=("act", 1))
                    self.bank_r_done(projb, "act", act_ev[m])
                j = it - 1
                if 0 <= j < Mb:
                    self.wait("tensor", "act", act_ev[j])
                    self.bank_w("tensor", tr1b)
                    tp = None
                    for half in range(2):
                        for t in range(8):
                            tp = self.op("tensor", lambda e, j=j, half=half, t=t: e.transpose(
                                self.psb(1, half)[:, t * 128:(t + 1) * 128],
                                qfm[j % 2][:, half * 1024 + t * 128:
                                           half * 1024 + (t + 1) * 128],
                                self.identb[:]),
                                inc=("pe", 1) if (half == 1 and t == 7) else None)
                    tr1_pe[j] = tp
                    self.bank_w_done(tr1b, "pe", tr1_pe[j])
                    self.bank_r("vector", tr1b)
                    if j >= 2 and need_kappa:
                        self.wait("vector", "act", ex_ev[j - 2])
                    ev_tm = None
                    for half in range(2):
                        ev_tm = self.op("vector", lambda e, j=j, half=half: e.tensor_copy(
                            qtm[j % 2][:, half * 1024:(half + 1) * 1024],
                            self.psb(1, half)), inc=("dve", 1) if half == 1 else None)
                    self.bank_r_done(tr1b, "dve", ev_tm)
                    if need_kappa:
                        h2 = HD // 2
                        qv = qtm[j % 2][:].rearrange("p (t h d) -> p t h d", t=NT, h=2)
                        kv = qkp[j % 2][:].rearrange("p (t h d) -> p t h d", t=NT, h=2)
                        cosv = self.cosD[:].rearrange("p (t d) -> p t d", t=NT)
                        sinv = self.sinsD[:].rearrange("p (t d) -> p t d", t=NT)
                        if j >= 2:
                            self.wait("vector", "pe", tr2_pe[j - 2])
                        self.op("vector", lambda e, qv=qv, kv=kv, sinv=sinv, h2=h2:
                                e.tensor_tensor(kv[:, :, :, 0:h2], qv[:, :, :, h2:HD],
                                sinv[:, :, 0:h2].unsqueeze(2).broadcast_to([128, NT, 2, h2]),
                                ALU.mult))
                        self.op("vector", lambda e, qv=qv, kv=kv, sinv=sinv, h2=h2:
                                e.tensor_tensor(kv[:, :, :, h2:HD], qv[:, :, :, 0:h2],
                                sinv[:, :, h2:HD].unsqueeze(2).broadcast_to([128, NT, 2, h2]),
                                ALU.mult))
                        self.op("vector", lambda e, qv=qv, cosv=cosv: e.tensor_tensor(
                            qv, qv,
                            cosv.unsqueeze(2).broadcast_to([128, NT, 2, HD]), ALU.mult))
                        self.op("vector", lambda e, j=j: e.tensor_tensor(
                            qkp[j % 2][:], qtm[j % 2][:], qkp[j % 2][:], ALU.add))
                        ev_mn = self.op("vector", lambda e, j=j: e.tensor_scalar_min(
                            qtm[j % 2][:], qkp[j % 2][:], 0.0), inc=("dve", 1))
                        self.wait("scalar", "dve", ev_mn)
                        ex_ev[j] = self.op("scalar", lambda e, j=j: e.activation(
                            qtm[j % 2][:], qtm[j % 2][:], AF.Exp), inc=("act", 1))
                        self.op("vector", lambda e, j=j: e.tensor_scalar_max(
                            qkp[j % 2][:], qkp[j % 2][:], 0.0))
                        self.wait("vector", "act", ex_ev[j])
                        stage1_ev[j] = self.op("vector", lambda e, j=j: e.tensor_tensor(
                            qkp[j % 2][:], qtm[j % 2][:], qkp[j % 2][:], ALU.add),
                            inc=("dve", 1))
                        tm_handler(j, qkp[j % 2], stage1_ev[j])
                    else:
                        stage1_ev[j] = ev_tm
                        tm_handler(j, qtm[j % 2], stage1_ev[j])
                i = it - 2
                if 0 <= i < Mb and fm_dst is not None:
                    src = qkp[i % 2] if need_kappa else qtm[i % 2]
                    self.wait("tensor", "dve", stage1_ev[i])
                    self.bank_w("tensor", tr2b)
                    tp = None
                    for half in range(2):
                        for t in range(8):
                            tp = self.op("tensor", lambda e, half=half, t=t, src=src:
                                         e.transpose(
                                self.psb(1, 2 + half)[:, t * 128:(t + 1) * 128],
                                src[:, half * 1024 + t * 128: half * 1024 + (t + 1) * 128],
                                self.identb[:]),
                                inc=("pe", 1) if (half == 1 and t == 7) else None)
                    tr2_pe[i] = tp
                    self.bank_w_done(tr2b, "pe", tr2_pe[i])
                    self.bank_r("vector", tr2b)
                    ev2 = None
                    for half in range(2):
                        ev2 = self.op("vector", lambda e, i=i, half=half: e.tensor_copy(
                            fm_dst(i)[:, half * 1024:(half + 1) * 1024],
                            self.psb(1, 2 + half)), inc=("dve", 1) if half == 1 else None)
                    self.bank_r_done(tr2b, "dve", ev2)
                elif 0 <= i < Mb:
                    tr2_pe[i] = self.cnt["pe"]
            return

        proj("wq_s", KH, lambda j, buf, ev: None,
             lambda i: Bb[:, i * S:(i + 1) * S], True, first=True)
        if stop == "q":
            return None
        for mch in range(KH):
            self.op("vector", lambda e, mch=mch: e.reduce_sum(
                qg_sb[:, mch:mch + 1], Bb[:, mch * S:(mch + 1) * S], axis=AX.X))
        ev_qg = self.op("vector", lambda e: e.engine_nop(), inc=("dve", 1))
        for n in range(NH):
            mch, half = n // 2, n % 2
            self.op("vector", lambda e, n=n, mch=mch, half=half: e.tensor_scalar_mul(
                qgl[0:64, n:n + 1], qg_sb[half * 64:half * 64 + 64, mch:mch + 1], 1.0 / S))
            self.op("vector", lambda e, n=n, mch=mch, half=half: e.tensor_scalar_mul(
                qgl[64:128, n:n + 1], qg_sb[half * 64:half * 64 + 64, mch:mch + 1], 1.0 / S))
        ev_qgl = self.op("vector", lambda e: e.engine_nop(), inc=("dve", 1))

        self.gate("gpsimd", "pe")
        self.wait("sync", "dve", ev_z)
        ev_cf = self.sync_dma(cosF[:].rearrange("p (t d) -> p t d", t=NT),
                              self.c_cosF.transpose([1, 0, 2]), "ld2")
        ev_cf = self.sync_dma(sinsF[:].rearrange("p (t d) -> p t d", t=NT),
                              self.c_sinsF.transpose([1, 0, 2]), "ld2")
        projb = [(0, s) for s in range(NS)]
        trb4 = [(1, s) for s in range(NS)]
        cosv = cosF[:].rearrange("p (t d) -> p t d", t=NT)
        sinv = sinsF[:].rearrange("p (t d) -> p t d", t=NT)
        h2 = HD // 2
        pe_kp = [None] * 4
        for m in range(4):
            par = m % 2
            if m >= 2:
                self.wait("gpsimd", "pe", pe_kp[m - 2])
            evw = self.cast_dma(wbuf[par][:],
                                din["wk_s"][m].rearrange("p k j -> p (k j)"),
                                f"wa{par}")
            self.wait("tensor", f"wa{par}", evw)
            self.bank_w("tensor", projb)
            pe_p = None
            for s in range(NS):
                for k in range(KH):
                    self.wait("tensor", *self.z_evs[k])
                    pe_p = self.op("tensor", lambda e, s=s, k=k, par=par: e.matmul(
                        self.ps(0, s), wbuf[par][:, k * 128:(k + 1) * 128],
                        zA[:, k * S + s * 512: k * S + (s + 1) * 512],
                        start=(k == 0), stop=(k == KH - 1)),
                        inc=("pe", 1) if (s == NS - 1 and k == KH - 1) else None)
            pe_kp[m] = pe_p
            self.bank_w_done(projb, "pe", pe_p)
            self.bank_r("scalar", projb)
            ev_evac = self.op("scalar", lambda e: e.copy(kfmA32[:], self.PS[0][:]),
                              inc=("act", 1))
            self.bank_r_done(projb, "act", ev_evac)
            self.wait("tensor", "act", ev_evac)
            self.bank_w("tensor", trb4)
            pe_t1 = None
            for t in range(NT):
                pe_t1 = self.op("tensor", lambda e, t=t: e.transpose(
                    self.PS[1][:, (t // 4) * 512 + (t % 4) * 128:
                               (t // 4) * 512 + (t % 4) * 128 + 128],
                    kfmA32[:, t * 128:(t + 1) * 128], self.identf[:]),
                    inc=("pe", 1) if t == NT - 1 else None)
            self.bank_w_done(trb4, "pe", pe_t1)
            self.bank_r("vector", trb4)
            ev_tm = None
            for b in range(4):
                ev_tm = self.op("vector", lambda e, b=b: e.tensor_copy(
                    ktm32[:, b * 512:(b + 1) * 512], self.ps(1, b)),
                    inc=("dve", 1) if b == 3 else None)
            self.bank_r_done(trb4, "dve", ev_tm)
            if stop == "k1" and m == 3:
                return None
            self.wait("vector", "ld2", ev_cf)
            kv32 = kap32[:].rearrange("p (t h d) -> p t h d", t=NT, h=2)
            qv32 = ktm32[:].rearrange("p (t h d) -> p t h d", t=NT, h=2)
            self.op("vector", lambda e, qv32=qv32, kv32=kv32: e.tensor_tensor(
                kv32[:, :, :, 0:h2], qv32[:, :, :, h2:HD],
                sinv[:, :, 0:h2].unsqueeze(2).broadcast_to([128, NT, 2, h2]),
                ALU.mult))
            self.op("vector", lambda e, qv32=qv32, kv32=kv32: e.tensor_tensor(
                kv32[:, :, :, h2:HD], qv32[:, :, :, 0:h2],
                sinv[:, :, h2:HD].unsqueeze(2).broadcast_to([128, NT, 2, h2]),
                ALU.mult))
            self.op("vector", lambda e, qv32=qv32: e.tensor_tensor(
                qv32, qv32, cosv.unsqueeze(2).broadcast_to([128, NT, 2, HD]),
                ALU.mult))
            self.op("vector", lambda e: e.tensor_tensor(
                kap32[:], ktm32[:], kap32[:], ALU.add))
            if stop == "k2" and m == 3:
                self.op("vector", lambda e: e.engine_nop(), inc=("dve", 1))
                return None
            ev_mn = self.op("vector", lambda e: e.tensor_scalar_min(
                ktm32[:], kap32[:], 0.0), inc=("dve", 1))
            self.wait("scalar", "dve", ev_mn)
            ev_ex = self.op("scalar", lambda e: e.activation(
                ktm32[:], ktm32[:], AF.Exp), inc=("act", 1))
            self.op("vector", lambda e: e.tensor_scalar_max(
                kap32[:], kap32[:], 0.0))
            self.wait("vector", "act", ev_ex)
            ev_kap = self.op("vector", lambda e: e.tensor_tensor(
                kap32[:], ktm32[:], kap32[:], ALU.add), inc=("dve", 1))
            dstk = kk_tm[:].rearrange("p (t c) -> p t c", t=NT)[:, :, m * 128:(m + 1) * 128]
            srck = kap32[:].rearrange("p (t c) -> p t c", t=NT)
            self.op("vector", lambda e, dstk=dstk, srck=srck: e.tensor_copy(dstk, srck))
            self.wait("tensor", "dve", ev_kap)
            self.bank_w("tensor", trb4)
            pe_t2 = None
            for t in range(NT):
                pe_t2 = self.op("tensor", lambda e, t=t: e.transpose(
                    self.PS[1][:, (t // 4) * 512 + (t % 4) * 128:
                               (t // 4) * 512 + (t % 4) * 128 + 128],
                    kap32[:, t * 128:(t + 1) * 128], self.identf[:]),
                    inc=("pe", 1) if (t % 4) == 3 else None)
            self.bank_w_done(trb4, "pe", pe_t2)
            self.bank_r("vector", trb4)
            ev_fb = None
            for b in range(4):
                ev_fb = self.op("vector", lambda e, b=b: e.tensor_copy(
                    kap32[:, b * 512:(b + 1) * 512], self.ps(1, b)),
                    inc=("dve", 1))
            self.bank_r_done(trb4, "dve", ev_fb)
            self.wait("tensor", "dve", ev_fb)
            self.wait("tensor", "dve", ev_qgl)
            if m == 0:
                self._altr_pe = 0
            for hh in range(2):
                g = 2 * m + hh
                self.bank_w("tensor", projb)
                pe_lg = None
                for s in range(NS):
                    pe_lg = self.op("tensor", lambda e, g=g, hh=hh, s=s: e.matmul(
                        self.ps(0, s)[0:4, :],
                        qgl[hh * 64:hh * 64 + 64, 4 * g:4 * g + 4],
                        kap32[hh * 64:hh * 64 + 64, s * 512:(s + 1) * 512],
                        start=True, stop=True),
                        inc=("pe", 1) if s == NS - 1 else None)
                self.bank_w_done(projb, "pe", pe_lg)
                self.bank_r("vector", projb)
                self.op("vector", lambda e: e.reduce_max(
                    nmx4[:], self.PS[0][0:4, :], axis=AX.X, negate=True))
                ev_nm = self.op("vector", lambda e: e.engine_nop(), inc=("dve", 1))
                self.wait("scalar", "dve", ev_nm)
                self.wait("scalar", "pe", self._altr_pe)
                self.bank_r("scalar", projb)
                ev_exp = self.op("scalar", lambda e: e.activation(
                    ex4[:], self.PS[0][0:4, :], AF.Exp, bias=nmx4[:]),
                    inc=("act", 1))
                self.bank_r_done(projb, "act", ev_exp)
                self.bank_r_done(projb, "dve", ev_nm)
                self.wait("vector", "act", ev_exp)
                self.op("vector", lambda e: e.reduce_sum(ssum4[:], ex4[:], axis=AX.X))
                self.fence_v()
                self.op("vector", lambda e: e.reciprocal(ssum4[:], ssum4[:]))
                self.fence_v()
                ev_al = self.op("vector", lambda e: e.tensor_scalar(
                    ex4[:], ex4[:], ssum4[:], float(S), ALU.mult, ALU.mult),
                    inc=("dve", 1))
                self.wait("tensor", "dve", ev_al)
                for t in range(NT):
                    self.bank_w("tensor", [(1, 0)])
                    ev_tr = self.op("tensor", lambda e, t=t: e.transpose(
                        self.psb(1, 0)[0:128, 0:4],
                        ex4[:, t * 128:(t + 1) * 128], self.identb[0:4, 0:4]),
                        inc=("pe", 1))
                    self.bank_w_done([(1, 0)], "pe", ev_tr)
                    self.bank_r("vector", [(1, 0)])
                    ev_a2d = self.op("vector", lambda e, t=t, g=g: e.tensor_copy(
                        al_tm[:, t * 32 + 4 * g: t * 32 + 4 * g + 4],
                        self.psb(1, 0)[0:128, 0:4]), inc=("dve", 1))
                    self.bank_r_done([(1, 0)], "dve", ev_a2d)
                    self._altr_pe = ev_tr
        if stop == "k":
            return None

        def v_handler(j, buf, ev):
            src = buf[:].rearrange("p (t c) -> p t c", t=NT)
            dstA = vtmA[:].rearrange("p (t c) -> p t c", t=8)[:, :, j * 128:(j + 1) * 128]
            dstB = vtmB[:].rearrange("p (t c) -> p t c", t=8)[:, :, j * 128:(j + 1) * 128]
            self.op("vector", lambda e, dstA=dstA, src=src: e.tensor_copy(
                dstA, src[:, 0:8, :]))
            self.op("vector", lambda e, dstB=dstB, src=src: e.tensor_copy(
                dstB, src[:, 8:16, :]), inc=("dve", 1))

        proj("wv_s", 4, v_handler, None, False)
        if stop == "v":
            return None

        if stop == "a2":
            return None
        ob = [(1, 2), (1, 3)]
        self.gate("vector", "pe")
        self.bank_w("tensor", ob)
        ev_ak = [None] * NT
        pe_ak = [None] * NT
        for t in range(NT):
            kkr = kk_tm[:, t * 512:(t + 1) * 512].rearrange(
                "p (kv d) -> p kv d", kv=NKV).unsqueeze(2).broadcast_to([128, NKV, 4, HD])
            alr = al_tm[:, t * 32:(t + 1) * 32].rearrange(
                "p (kv r) -> p kv r", kv=NKV).unsqueeze(3).broadcast_to([128, NKV, 4, HD])
            akv = ak_sb[t % 2][:].rearrange("p (kv r d) -> p kv r d", kv=NKV, r=4)
            if t >= 2:
                self.wait("vector", "pe", pe_ak[t - 2])
            ev_ak[t] = self.op("vector", lambda e, akv=akv, kkr=kkr, alr=alr:
                               e.tensor_tensor(akv, kkr, alr, ALU.mult), inc=("dve", 1))
            self.wait("tensor", "dve", ev_ak[t])
            for n in range(NH):
                pair, base = n // 2, (n % 2) * 64
                kvh = n // 4
                st_f = (t == 0) and (n == 0 or n == 16)
                sp_f = (t == NT - 1) and (n == 15 or n == 31)
                pe_ak[t] = self.op("tensor", lambda e, t=t, n=n, pair=pair, base=base,
                                   kvh=kvh, st_f=st_f, sp_f=sp_f: e.matmul(
                    self.PS[1][base:base + 64, 1024 + pair * 64: 1024 + pair * 64 + 64],
                    ak_sb[t % 2][:, n * HD:(n + 1) * HD],
                    vt_ap(t, kvh * HD, HD),
                    start=st_f, stop=sp_f, skip_group_check=True),
                    inc=("pe", 1) if n == NH - 1 else None)
        self.bank_w_done(ob, "pe", pe_ak[NT - 1])
        self.bank_r("vector", ob)
        ev_outer = None
        for half in range(2):
            ev_outer = self.op("vector", lambda e, half=half: e.tensor_copy(
                outer_sb[:, half * 512:(half + 1) * 512],
                self.PS[1][:, 1024 + half * 512: 1024 + (half + 1) * 512]),
                inc=("dve", 1))
        self.bank_r_done(ob, "dve", ev_outer)

        if stop == "a3":
            return None
        self.gate("gpsimd", "pe", "dve", "act")
        resb = [(0, s) for s in range(NS)]
        phib = [(1, s) for s in range(NS)]
        evw = [None] * KH
        pe_phi = [None] * KH
        for m in range(KH):
            par = m % 2
            if m >= 2:
                self.wait("gpsimd", "pe", pe_phi[m - 2])
            evw[m] = self.cast_dma(wbuf[par][:],
                                   din["wphi_s"][m].rearrange("p k j -> p (k j)"),
                                   f"wa{par}")
            self.wait("tensor", "dve", ev_outer)
            self.bank_w("tensor", resb)
            pe_res = None
            for hh in range(2):
                n = 2 * m + hh
                base = hh * 64
                for s in range(NS):
                    pe_res = self.op("tensor", lambda e, m=m, n=n, base=base, s=s: e.matmul(
                        self.ps(0, s)[base:base + 64, :],
                        outer_sb[base:base + 64, m * 64:(m + 1) * 64],
                        Bb[base:base + 64, m * S + s * 512: m * S + (s + 1) * 512],
                        start=True, stop=True),
                        inc=("pe", 1) if (hh == 1 and s == NS - 1) else None)
            self.bank_w_done(resb, "pe", pe_res)
            self.bank_r("scalar", resb)
            if m >= 2:
                self.wait("scalar", "dve", ctx_ev[m - 2])
            ev_resev = self.op("scalar", lambda e, m=m: e.copy(
                res_sb[m % 2][:], self.PS[0][:]), inc=("act", 1))
            self.bank_r_done(resb, "act", ev_resev)
            self.wait("tensor", f"wa{par}", evw[m])
            self.bank_w("tensor", phib)
            for s in range(NS):
                for k in range(KH):
                    pe_phi[m] = self.op("tensor", lambda e, s=s, k=k, par=par: e.matmul(
                        self.ps(1, s), wbuf[par][:, k * 128:(k + 1) * 128],
                        zA[:, k * S + s * 512: k * S + (s + 1) * 512],
                        start=(k == 0), stop=(k == KH - 1)),
                        inc=("pe", 1) if (s == NS - 1 and k == KH - 1) else None)
            self.bank_w_done(phib, "pe", pe_phi[m])
            self.bank_r("vector", phib)
            self.wait("vector", "act", ev_resev)
            if m == 0:
                ctx_ev = [None] * KH
            ctx_ev[m] = self.op("vector", lambda e, m=m: e.tensor_tensor(
                Bb[:, m * S:(m + 1) * S], self.PS[1][:], res_sb[m % 2][:], ALU.mult),
                inc=("dve", 1))
            self.bank_r_done(phib, "dve", ctx_ev[m])
        ev_ctx_all = self.cnt["dve"]

        if stop == "a4":
            return None
        self.gate("sync", "pe", "dve")
        self.gate("gpsimd", "pe")
        wob = [(0, s) for s in range(NS)]
        evw = [None] * KH
        pe_wo = [None] * KH
        xo_ev = [None] * KH
        st_ev = [None] * KH
        for m in range(KH):
            par = m % 2
            if m >= 2:
                self.wait("gpsimd", "pe", pe_wo[m - 2])
            evw[m] = self.cast_dma(wbuf[par][:],
                                   din["wo_s"][m].rearrange("p k j -> p (k j)"),
                                   f"wa{par}")
            if m >= 2:
                self.wait("sync", "dve", xo_ev[m - 2])
            evx = self.sync_dma(xt_sb[par][:], x_src[m * 128:(m + 1) * 128, :],
                                f"ldx{par}")
            self.wait("tensor", f"wa{par}", evw[m])
            self.wait("tensor", "dve", ev_ctx_all)
            self.bank_w("tensor", wob)
            for s in range(NS):
                for k in range(KH):
                    pe_wo[m] = self.op("tensor", lambda e, s=s, k=k, par=par: e.matmul(
                        self.ps(0, s), wbuf[par][:, k * 128:(k + 1) * 128],
                        Bb[:, k * S + s * 512: k * S + (s + 1) * 512],
                        start=(k == 0), stop=(k == KH - 1)),
                        inc=("pe", 1) if (s == NS - 1 and k == KH - 1) else None)
            self.bank_w_done(wob, "pe", pe_wo[m])
            self.bank_r("vector", wob)
            self.wait("vector", f"ldx{par}", evx)
            if m >= 2:
                self.wait("vector", f"stx{par}", st_ev[m - 2])
            xo_ev[m] = self.op("vector", lambda e, par=par: e.tensor_tensor(
                xo_sb[par][:], self.PS[0][:], xt_sb[par][:], ALU.add), inc=("dve", 1))
            self.bank_r_done(wob, "dve", xo_ev[m])
            self.wait("sync", "dve", xo_ev[m])
            st_ev[m] = self.sync_dma(x_dst[m * 128:(m + 1) * 128, :], xo_sb[par][:],
                                     f"stx{par}")
        return st_ev

    def emit_ffn(self, x_src, x_dst, ev_z, pg=None):
        C = self.C_u8
        din = self.din
        zA, Bb = self.zA, self.Bb
        w1b = [self.cv(C, i * 4096, 2048, BF16) for i in range(2)]
        w3b = [self.cv(C, 8192 + i * 4096, 2048, BF16) for i in range(2)]
        b1b = [self.cv(C, 16384 + i * 512, 256, BF16) for i in range(2)]
        b3b = [self.cv(C, 17408 + i * 512, 256, BF16) for i in range(2)]
        a2b = [self.cv(C, 18432 + i * 512, 256, BF16) for i in range(G_A2)]
        wgb = self.cv(C, 19968, 128, BF16)
        x2st = [self.cv(C, 20480 + i * 4096, 2048, BF16) for i in range(2)]
        s1b = [self.cv(C, 28672 + i * 1024, 512, BF16) for i in range(2)]
        prob = self.cv(C, 30720, 128, F32)
        pmm = self.cv(C, 31232, 128, F32)
        m1c = self.cv(C, 31744, 128, F32)
        p1c = self.cv(C, 32256, 16, F32)
        p2c = self.cv(C, 32320, 16, F32)
        rw0 = self.cv(C, 32384, 16, F32)
        rw1 = self.cv(C, 32448, 16, F32)
        m1T = self.cv(C, 32768 if False else 32512, 2048, BF16, rows=NE)
        m2T = self.cv(C, 36608, 2048, BF16, rows=NE)
        rwrow = [self.cv(C, 40960 + i * 8192, 2048, F32, rows=1) for i in range(2)]
        mex = [self.cv(C, 40960 + i * 4096, 2048, BF16) for i in range(2)]
        strm = self.cv(C, 57344, 4096, BF16)
        a2acc = [self.cv(C, 40960 + i * 16384, 4096, F32) for i in range(2)]
        dnma2 = [self.cv(C, i * 4096, 2048, BF16) for i in range(4)]
        ma2mex = self.cv(C, 16384, 2048, BF16)
        b2sb = self.cv(C, 20480, 4096, BF16)
        w2g2 = [self.cv(C, 28672 + i * 16384, G_DN * 1024, BF16) for i in range(2)]
        xod = [self.cv(C, 16384, 1024, F32),
               self.cv(C, 61440, 1024, F32)]
        xtd = [self.cv(C, 65536 + i * 4096, 1024, F32) for i in range(2)]
        xtd2 = [self.cv(self.B_u8, i * 4096, 1024, F32) for i in range(2)]
        ring = [Bb[:, i * 2048:(i + 1) * 2048] for i in range(2 * G_A2)]
        off = 2 * G_A2
        ma1 = [Bb[:, (off + i) * 2048:(off + i + 1) * 2048] for i in range(2)]
        d1v = [Bb[:, (off + 2 + i) * 2048:(off + 3 + i) * 2048] for i in range(2)]
        ma3v = [Bb[:, (off + 4 + i) * 2048:(off + 5 + i) * 2048] for i in range(2)]
        d3v = [Bb[:, (off + 6 + i) * 2048:(off + 7 + i) * 2048] for i in range(2)]
        rwb = [Bb[:, (off + 8 + i) * 2048:(off + 9 + i) * 2048] for i in range(2)]
        dnring2 = [[Bb[:, (p * G_DN + i) * 2048:(p * G_DN + i + 1) * 2048]
                    for i in range(G_DN)] for p in range(2)]

        pre = getattr(self, "_ffn_pre", None)
        if pre is not None:
            ev_wg = pre[0]
        else:
            if pg is not None:
                self.gate_vals("gpsimd", pg)
            else:
                self.gate("gpsimd", "pe", "dve", "act")
                self.gate_stores("gpsimd")
            ev_wg = self.cast_dma(wgb[:].rearrange("p (k c) -> p k c", k=KH),
                                  din["wg_t"].rearrange("(k p) c -> p k c", p=128),
                                  "msc")
        gb0, gb1 = (1, 0), (1, 1)
        trb = [(1, 2)]
        self.wait("tensor", "msc", ev_wg)
        last_tr = None
        for t in range(NT):
            bsel = gb0 if t % 2 == 0 else gb1
            pcol = prob[:, t * 8:(t + 1) * 8]
            mcol = pmm[:, t * 8:(t + 1) * 8]
            ccol = m1c[:, t * 8:(t + 1) * 8]
            self.bank_w("tensor", [bsel])
            pe_g = None
            for k in range(KH):
                self.wait("tensor", *self.z_evs[k])
                pe_g = self.op("tensor", lambda e, t=t, k=k, bsel=bsel: e.matmul(
                    self.PS[bsel[0]][:, bsel[1] * 512: bsel[1] * 512 + 8],
                    zA[:, k * S + t * 128: k * S + (t + 1) * 128],
                    wgb[:, k * 8:(k + 1) * 8],
                    start=(k == 0), stop=(k == KH - 1)),
                    inc=("pe", 1) if k == KH - 1 else None)
            self.bank_w_done([bsel], "pe", pe_g)
            self.bank_r("vector", [bsel])
            self.op("vector", lambda e, t=t, bsel=bsel: e.reduce_max(
                p1c[:, t:t + 1], self.PS[bsel[0]][:, bsel[1] * 512: bsel[1] * 512 + 8],
                axis=AX.X, negate=True))
            ev0 = self.op("vector", lambda e: e.engine_nop(), inc=("dve", 1))
            self.wait("scalar", "dve", ev0)
            ev_e = self.op("scalar", lambda e, t=t, bsel=bsel, pcol=pcol: e.activation(
                pcol, self.PS[bsel[0]][:, bsel[1] * 512: bsel[1] * 512 + 8],
                AF.Exp, bias=p1c[:, t:t + 1]), inc=("act", 1))
            self.bank_r_done([bsel], "act", ev_e)
            self.wait("vector", "act", ev_e)
            self.op("vector", lambda e, t=t, pcol=pcol: e.reduce_sum(
                p2c[:, t:t + 1], pcol, axis=AX.X))
            self.fence_v()
            self.op("vector", lambda e, t=t: e.reciprocal(p2c[:, t:t + 1], p2c[:, t:t + 1]))
            self.fence_v()
            self.op("vector", lambda e, t=t, pcol=pcol: e.tensor_scalar_mul(
                pcol, pcol, p2c[:, t:t + 1]))
            self.fence_v()
            self.op("vector", lambda e, t=t, pcol=pcol: e.reduce_max(
                p1c[:, t:t + 1], pcol, axis=AX.X))
            self.fence_v()
            self.op("vector", lambda e, t=t, pcol=pcol, mcol=mcol: e.tensor_scalar(
                mcol, pcol, p1c[:, t:t + 1], None, ALU.is_equal))
            self.fence_v()
            self.op("vector", lambda e, mcol=mcol, ccol=ccol: e.tensor_scalar(
                ccol, mcol, -1.0, 1.0, ALU.mult, ALU.add))
            self.fence_v()
            self.op("vector", lambda e, pcol=pcol, ccol=ccol: e.tensor_tensor(
                ccol, pcol, ccol, ALU.mult))
            self.fence_v()
            self.op("vector", lambda e, t=t, ccol=ccol: e.reduce_max(
                p2c[:, t:t + 1], ccol, axis=AX.X))
            self.fence_v()
            self.op("vector", lambda e, t=t: e.tensor_tensor(
                rw0[:, t:t + 1], p1c[:, t:t + 1], p2c[:, t:t + 1], ALU.add))
            self.fence_v()
            self.op("vector", lambda e, t=t: e.reciprocal(rw0[:, t:t + 1], rw0[:, t:t + 1]))
            self.fence_v()
            self.op("vector", lambda e, t=t: e.tensor_tensor(
                rw1[:, t:t + 1], p2c[:, t:t + 1], rw0[:, t:t + 1], ALU.mult))
            self.fence_v()
            self.op("vector", lambda e, t=t: e.tensor_tensor(
                rw0[:, t:t + 1], p1c[:, t:t + 1], rw0[:, t:t + 1], ALU.mult))
            self.fence_v()
            ev_m = self.op("vector", lambda e, t=t, ccol=ccol: e.tensor_scalar(
                ccol, ccol, p2c[:, t:t + 1], None, ALU.is_equal), inc=("dve", 1))
            self.wait("tensor", "dve", ev_m)
            for src, dstT in ((mcol, m1T), (ccol, m2T)):
                self.bank_w("tensor", trb)
                ev_tr = self.op("tensor", lambda e, src=src: e.transpose(
                    self.ps(1, 2)[0:8, 0:128], src, self.identf[:]), inc=("pe", 1))
                self.bank_w_done(trb, "pe", ev_tr)
                self.bank_r("vector", trb)
                ev_cp = self.op("vector", lambda e, t=t, dstT=dstT: e.tensor_copy(
                    dstT[:, t * 128:(t + 1) * 128], self.ps(1, 2)[0:8, 0:128]),
                    inc=("dve", 1))
                self.bank_r_done(trb, "dve", ev_cp)
            for src, row in ((rw0, rwrow[0]), (rw1, rwrow[1])):
                self.bank_w("tensor", trb)
                ev_tr = self.op("tensor", lambda e, t=t, src=src: e.transpose(
                    self.ps(1, 2)[0:1, 0:128], src[:, t:t + 1], self.identf[:]),
                    inc=("pe", 1))
                self.bank_w_done(trb, "pe", ev_tr)
                self.bank_r("vector", trb)
                last_tr = self.op("vector", lambda e, t=t, row=row: e.tensor_copy(
                    row[:, t * 128:(t + 1) * 128], self.ps(1, 0)[0:1, 1024:1152]
                    if False else self.ps(1, 2)[0:1, 0:128]), inc=("dve", 1))
                self.bank_r_done(trb, "dve", last_tr)
        bb = [(0, s) for s in range(NS)]
        for i in range(2):
            self.wait("tensor", "dve", last_tr)
            self.bank_w("tensor", bb)
            pe_b = None
            for s in range(NS):
                pe_b = self.op("tensor", lambda e, i=i, s=s: e.matmul(
                    self.ps(0, s), self.ones1x[:], rwrow[i][:, s * 512:(s + 1) * 512],
                    start=True, stop=True), inc=("pe", 1) if s == NS - 1 else None)
            self.bank_w_done(bb, "pe", pe_b)
            self.bank_r("scalar", bb)
            ev_e = self.op("scalar", lambda e, i=i: e.copy(rwb[i][:], self.PS[0][:]),
                           inc=("act", 1))
            self.bank_r_done(bb, "act", ev_e)

        ab = [(0, s) for s in range(NS)]
        meb = [(1, s) for s in range(NS)]
        pe_a2g = {}

        def up_dmas(m):
            par = m % 2
            g = m // G_A2
            gs = m % G_A2
            self.cast_dma(w1b[par][:],
                          din["w1_s"][m].rearrange("p k j -> p (k j)"), f"wu{par}")
            self.cast_dma(w3b[par][:],
                          din["w3_s"][m].rearrange("p k j -> p (k j)"), f"wu{par}")
            self.cast_dma(b1b[par][:].rearrange("p (k j) -> p k j", k=2),
                          din["b1s_s"][m], f"wu{par}")
            self.cast_dma(b3b[par][:].rearrange("p (k j) -> p k j", k=2),
                          din["b3s_s"][m], f"wu{par}")
            if gs == 0 and g > 0:
                self.wait("gpsimd", "pe", pe_a2g[g - 1])
            return self.cast_dma(a2b[gs][:], din["a2st"][m * 128:(m + 1) * 128, :],
                                 f"wu{par}")
        if pre is not None:
            ev_pre = pre[1]
            self._ffn_pre = None
        else:
            ev_pre = [up_dmas(0), up_dmas(1)]
        self.wait_zall("gpsimd")
        self.gate_stores("gpsimd")
        ev_p2 = None
        for which, skey, madst, ddst in ((0, "a1st", ma1, d1v), (1, "a3st", ma3v, d3v)):
            ev_s = self.cast_dma(strm[:].rearrange("p (m k j) -> p m k j", m=2, k=KH),
                                 din[skey].transpose([1, 0, 2, 3]), "msc")
            for er in range(2):
                for mi, mT in ((0, m1T), (1, m2T)):
                    self.bank_w("tensor", meb)
                    pe_me = None
                    for s in range(NS):
                        pe_me = self.op("tensor", lambda e, er=er, s=s, mT=mT: e.matmul(
                            self.ps(1, s), self.s8[:, er * 128:(er + 1) * 128],
                            mT[:, s * 512:(s + 1) * 512], start=True, stop=True),
                            inc=("pe", 1) if s == NS - 1 else None)
                    self.bank_w_done(meb, "pe", pe_me)
                    self.bank_r("scalar", meb)
                    if not (which == 0 and er == 0 and mi == 0):
                        self.wait("scalar", "dve", ev_p2)
                    ev_me = self.op("scalar", lambda e, mi=mi: e.copy(
                        mex[mi][:], self.PS[1][:]), inc=("act", 1))
                    self.bank_r_done(meb, "act", ev_me)
                self.wait("tensor", "msc", ev_s)
                self.bank_w("tensor", ab)
                pe_a = None
                for s in range(NS):
                    for k in range(KH):
                        self.wait("tensor", *self.z_evs[k])
                        pe_a = self.op("tensor", lambda e, er=er, s=s, k=k: e.matmul(
                            self.ps(0, s),
                            strm[:, er * 2048 + k * 128: er * 2048 + (k + 1) * 128],
                            zA[:, k * S + s * 512: k * S + (s + 1) * 512],
                            start=(k == 0), stop=(k == KH - 1)),
                            inc=("pe", 1) if (s == NS - 1 and k == KH - 1) else None)
                self.bank_w_done(ab, "pe", pe_a)
                self.bank_r("vector", ab)
                self.wait("vector", "act", ev_me)
                self.op("vector", lambda e, madst=madst, er=er: e.tensor_tensor(
                    madst[er][:], self.PS[0][:], mex[0][:], ALU.mult))
                self.op("vector", lambda e, ddst=ddst, er=er: e.tensor_tensor(
                    ddst[er][:], self.PS[0][:], mex[1][:], ALU.mult))
                ev_p2 = self.op("vector", lambda e, madst=madst, ddst=ddst, er=er:
                                e.tensor_tensor(ddst[er][:], ddst[er][:], madst[er][:],
                                                ALU.subtract), inc=("dve", 1))
                self.bank_r_done(ab, "dve", ev_p2)

        x1b = lambda s: (0, s % 2)
        x3b = lambda s: (0, 2 + s % 2)
        a2bk = [(1, 2), (1, 3)]
        pe_up = [None] * MF
        st_x2c = [None] * MF
        self.wait("vector", "plr", self.cnt["plr"])
        dve_fold = [None] * MF
        for m in range(MF):
            par = m % 2
            g = m // G_A2
            gs = m % G_A2
            if m >= 2:
                self.wait("gpsimd", "pe", pe_up[m - 2])
                ev_grp = up_dmas(m)
            else:
                ev_grp = ev_pre[m]
            self.wait("tensor", f"wu{par}", ev_grp)
            if m == 0:
                self.wait_zall("tensor")
            if gs == 0 and g > 0:
                self.wait("vector", "pe", pe_a2g[g - 1])
            evacA = [None] * NS
            for s in range(NS):
                self.bank_w("tensor", [x1b(s)])
                pe1 = None
                for k in range(KH):
                    self.op("tensor", lambda e, s=s, k=k, par=par: e.matmul(
                        self.ps(*x1b(s)), w1b[par][:, k * 128:(k + 1) * 128],
                        zA[:, k * S + s * 512: k * S + (s + 1) * 512],
                        start=(k == 0), stop=False))
                for er in range(2):
                    pe1 = self.op("tensor", lambda e, s=s, er=er, par=par: e.matmul(
                        self.ps(*x1b(s)), b1b[par][:, er * 128:(er + 1) * 128],
                        ma1[er][:, s * 512:(s + 1) * 512],
                        start=False, stop=(er == 1)), inc=("pe", 1) if er == 1 else None)
                self.bank_w_done([x1b(s)], "pe", pe1)
                self.bank_r("scalar", [x1b(s)])
                evacA[s] = self.op("scalar", lambda e, s=s: e.activation(
                    s1b[s % 2][:], self.ps(*x1b(s)), AF.Silu), inc=("act", 1))
                self.bank_r_done([x1b(s)], "act", evacA[s])
            for s in range(NS):
                self.bank_w("tensor", [x3b(s)])
                pe3 = None
                for k in range(KH):
                    self.op("tensor", lambda e, s=s, k=k, par=par: e.matmul(
                        self.ps(*x3b(s)), w3b[par][:, k * 128:(k + 1) * 128],
                        zA[:, k * S + s * 512: k * S + (s + 1) * 512],
                        start=(k == 0), stop=False))
                for er in range(2):
                    pe3 = self.op("tensor", lambda e, s=s, er=er, par=par: e.matmul(
                        self.ps(*x3b(s)), b3b[par][:, er * 128:(er + 1) * 128],
                        ma3v[er][:, s * 512:(s + 1) * 512],
                        start=False, stop=(er == 1)), inc=("pe", 1) if er == 1 else None)
                self.bank_w_done([x3b(s)], "pe", pe3)
                self.bank_r("vector", [x3b(s)])
                self.wait("vector", "act", evacA[s])
                ev = self.op("vector", lambda e, s=s, gs=gs: e.tensor_tensor(
                    ring[2 * gs][:, s * 512:(s + 1) * 512], s1b[s % 2][:],
                    self.ps(*x3b(s)), ALU.mult), inc=("dve", 1))
                self.bank_r_done([x3b(s)], "dve", ev)
            evacA1 = [None] * NS
            for s in range(NS):
                self.bank_w("tensor", [x1b(s)])
                pe1 = None
                for er in range(2):
                    pe1 = self.op("tensor", lambda e, s=s, er=er, par=par: e.matmul(
                        self.ps(*x1b(s)), b1b[par][:, er * 128:(er + 1) * 128],
                        d1v[er][:, s * 512:(s + 1) * 512],
                        start=False, stop=(er == 1), skip_group_check=True),
                        inc=("pe", 1) if er == 1 else None)
                self.bank_w_done([x1b(s)], "pe", pe1)
                self.bank_r("scalar", [x1b(s)])
                evacA1[s] = self.op("scalar", lambda e, s=s: e.activation(
                    s1b[s % 2][:], self.ps(*x1b(s)), AF.Silu), inc=("act", 1))
                self.bank_r_done([x1b(s)], "act", evacA1[s])
            for s in range(NS):
                self.bank_w("tensor", [x3b(s)])
                pe3 = None
                for er in range(2):
                    pe3 = self.op("tensor", lambda e, s=s, er=er, par=par: e.matmul(
                        self.ps(*x3b(s)), b3b[par][:, er * 128:(er + 1) * 128],
                        d3v[er][:, s * 512:(s + 1) * 512],
                        start=False, stop=(er == 1), skip_group_check=True),
                        inc=("pe", 1) if er == 1 else None)
                self.bank_w_done([x3b(s)], "pe", pe3)
                self.bank_r("vector", [x3b(s)])
                self.wait("vector", "act", evacA1[s])
                ev = self.op("vector", lambda e, s=s, gs=gs: e.tensor_tensor(
                    ring[2 * gs + 1][:, s * 512:(s + 1) * 512], s1b[s % 2][:],
                    self.ps(*x3b(s)), ALU.mult), inc=("dve", 1))
                self.bank_r_done([x3b(s)], "dve", ev)
            self.op("vector", lambda e, gs=gs: e.tensor_tensor(
                ring[2 * gs][:], ring[2 * gs][:], rwb[0][:], ALU.mult))
            ev_scale2 = self.op("vector", lambda e, gs=gs: e.tensor_tensor(
                ring[2 * gs + 1][:], ring[2 * gs + 1][:], rwb[1][:], ALU.mult),
                inc=("dve", 1))
            if m >= 2:
                self.wait("vector", f"stc{par}", st_x2c[m - 2])
            dve_fold[m] = self.op("vector", lambda e, gs=gs, par=par: e.tensor_tensor(
                x2st[par][:], ring[2 * gs][:], ring[2 * gs + 1][:], ALU.add),
                inc=("dve", 1))
            self.wait("sync", "dve", dve_fold[m])
            st_x2c[m] = self.sync_dma(self.x2c_s[m], x2st[par][:], f"stc{par}")
            pe_up[m] = self.cnt["pe"]
            if gs == G_A2 - 1 or m == MF - 1:
                nmm = gs + 1
                self.wait("tensor", "dve", ev_scale2)
                for kk in range(2):
                    for er in range(2):
                        for s in range(NS):
                            bank = a2bk[(er * NS + s) % 2]
                            self.bank_w("tensor", [bank])
                            pe_t = None
                            for mi in range(nmm):
                                pe_t = self.op("tensor", lambda e, mi=mi, er=er, s=s,
                                               kk=kk, bank=bank, nmm=nmm: e.matmul(
                                    self.ps(*bank),
                                    a2b[mi][:, er * 128:(er + 1) * 128],
                                    ring[2 * mi + kk][:, s * 512:(s + 1) * 512],
                                    start=(mi == 0), stop=(mi == nmm - 1)),
                                    inc=("pe", 1) if mi == nmm - 1 else None)
                            self.bank_w_done([bank], "pe", pe_t)
                            self.bank_r("vector", [bank])
                            dst = a2acc[kk][:, er * 2048 + s * 512:
                                            er * 2048 + (s + 1) * 512]
                            tag = f"a2acc g{g} kk{kk} er{er} s{s} bank{bank}"
                            if g == 0:
                                ev_a = self.op("vector", lambda e, dst=dst, bank=bank,
                                               tag=tag: e.tensor_copy(
                                                   dst, self.ps(*bank)).annotate(tag),
                                               inc=("dve", 1))
                            else:
                                ev_a = self.op("vector", lambda e, dst=dst, bank=bank,
                                               tag=tag: e.tensor_tensor(
                                                   dst, dst, self.ps(*bank),
                                                   ALU.add).annotate(tag), inc=("dve", 1))
                            self.bank_r_done([bank], "dve", ev_a)
                pe_a2g[g] = self.cnt["pe"]
        ev_up_dve = self.cnt["dve"]
        pe_after_up = self.cnt["pe"]

        meb2 = [(1, s) for s in range(NS)]
        self.gate("gpsimd", "pe", "dve", "act")
        self.wait("vector", "plr", self.cnt["plr"])
        for kk in range(2):
            mT = m1T if kk == 0 else m2T
            for er in range(2):
                self.bank_w("tensor", meb2)
                pe_me = None
                for s in range(NS):
                    pe_me = self.op("tensor", lambda e, er=er, s=s, mT=mT: e.matmul(
                        self.ps(1, s), self.s8[:, er * 128:(er + 1) * 128],
                        mT[:, s * 512:(s + 1) * 512], start=True, stop=True),
                        inc=("pe", 1) if s == NS - 1 else None)
                self.bank_w_done(meb2, "pe", pe_me)
                self.bank_r("scalar", meb2)
                if not (kk == 0 and er == 0):
                    self.wait("scalar", "dve", ev_ma2)
                ev_mecp = self.op("scalar", lambda e: e.copy(ma2mex[:], self.PS[1][:]),
                                  inc=("act", 1))
                self.bank_r_done(meb2, "act", ev_mecp)
                self.wait("vector", "act", ev_mecp)
                self.wait("vector", "dve", ev_up_dve)
                ev_ma2 = self.op("vector", lambda e, kk=kk, er=er: e.tensor_tensor(
                    dnma2[kk * 2 + er][:], a2acc[kk][:, er * 2048:(er + 1) * 2048],
                    ma2mex[:], ALU.mult), inc=("dve", 1))
        ev_ma2_done = self.cnt["dve"]
        self.gate_stores("gpsimd")
        self.wait("gpsimd", "dve", ev_ma2_done)
        ev_b2 = self.cast_dma(b2sb[:].rearrange("p (k c) -> p k c", k=2),
                              din["b2s"].rearrange("(k p) c -> p k c", p=128), "msc")
        dnb = [(0, 0), (0, 1), (0, 2), (0, 3)]
        st_x = [None] * 32
        self.wait("sync", "pe", pe_after_up)
        self.wait("sync", "dve", ev_up_dve)
        self.gate_stores("sync")
        gplan = [(0, 4), (4, 4)] + [(8 + 8 * i, 8) for i in range((MF - 8) // 8)]
        groups = [(half, m0, nmm) for half in range(2) for m0, nmm in gplan]
        grp_pe_end = {}
        grp_ld = {}
        res_pl = {0: [], 1: []}
        res_gate = {}

        def load_group(gi):
            half, m0, nmm = groups[gi]
            gp = gi % 2
            if gi >= 2:
                self.wait("sync", "pe", grp_pe_end[gi - 2])
                self.wait("gpsimd", "pe", grp_pe_end[gi - 2])
            evr = evw2 = None
            for mi in range(nmm):
                m = m0 + mi
                evr = self.sync_dma(dnring2[gp][mi][:], self.x2c_s[m], "rng")
                evw2 = self.cast_dma(
                    w2g2[gp][:, mi * 1024:(mi + 1) * 1024],
                    din["w2t"][m * 128:(m + 1) * 128,
                               half * 1024:(half + 1) * 1024], "w2s")
            grp_ld[gi] = (evr, evw2)

        load_group(0)
        load_group(1)
        ngh = len(gplan)
        for gi, (half, m0, nmm) in enumerate(groups):
            gp = gi % 2
            dnring = dnring2[gp]
            w2g = w2g2[gp]
            first_grp = (m0 == 0)
            last_grp = (gi % ngh == ngh - 1)
            self.wait("tensor", "w2s", grp_ld[gi][1])
            self.wait("tensor", "rng", grp_ld[gi][0])
            if last_grp:
                self.wait("tensor", "msc", ev_b2)
                self.wait("tensor", "dve", ev_ma2_done)
            for h in range(8):
                for s in range(NS):
                    bank = dnb[(h * NS + s) % 4]
                    self.bank_w("tensor", [bank])
                    pe_t = None
                    for mi in range(nmm):
                        lastmm = mi == nmm - 1
                        pe_t = self.op("tensor", lambda e, mi=mi, h=h, s=s, bank=bank,
                                nmm=nmm, lastmm=lastmm, w2g=w2g, dnring=dnring: e.matmul(
                            self.ps(*bank),
                            w2g[:, mi * 1024 + h * 128: mi * 1024 + (h + 1) * 128],
                            dnring[mi][:, s * 512:(s + 1) * 512],
                            start=(mi == 0), stop=lastmm),
                            inc=("pe", 1) if (lastmm and not last_grp) else None)
                    if last_grp:
                        hg = half * 8 + h
                        for kk in range(2):
                            for er in range(2):
                                lastmm = (kk == 1 and er == 1)
                                pe_t = self.op("tensor", lambda e, kk=kk, er=er, hg=hg,
                                        s=s, bank=bank, lastmm=lastmm: e.matmul(
                                    self.ps(*bank),
                                    b2sb[:, er * 2048 + hg * 128:
                                         er * 2048 + (hg + 1) * 128],
                                    dnma2[kk * 2 + er][:, s * 512:(s + 1) * 512],
                                    start=False, stop=lastmm,
                                    skip_group_check=True),
                                    inc=("pe", 1) if lastmm else None)
                    self.bank_w_done([bank], "pe", pe_t)
                    self.bank_r("vector", [bank])
                    dst = self.accA[:, h * 2048 + s * 512: h * 2048 + (s + 1) * 512]
                    if first_grp:
                        if half == 1:
                            self.wait("vector", "plr", res_pl[0][2 * h + 1])
                        ev_a = self.op("vector", lambda e, dst=dst, bank=bank:
                                       e.tensor_copy(dst, self.ps(*bank)),
                                       inc=("dve", 1))
                    else:
                        ev_a = self.op("vector", lambda e, dst=dst, bank=bank:
                                       e.tensor_tensor(dst, dst, self.ps(*bank),
                                                       ALU.add), inc=("dve", 1))
                    self.bank_r_done([bank], "dve", ev_a)
                    if last_grp and s == NS - 1:
                        res_gate[h] = ev_a
            grp_pe_end[gi] = self.cnt["pe"]
            if gi + 2 < len(groups):
                load_group(gi + 2)
            if not last_grp:
                continue
            prev_tt = getattr(self, "_res_prev_tt", None)
            prev_st = getattr(self, "_res_prev_st", None)
            nbuf = 2 if half == 0 else 4
            xtd_l = xtd if half == 0 else [xtd[0], xtd[1], xtd2[0], xtd2[1]]
            idx = 0
            xo_ev = {}
            st_half = [None] * 16
            for h in range(8):
                hg = half * 8 + h
                for c2 in range(2):
                    par = idx % 2
                    bpar = idx % nbuf
                    if idx >= nbuf:
                        self.wait("sync", "plr", xo_ev[idx - nbuf])
                    elif bpar < 2 and prev_tt is not None:
                        self.wait("sync", "plr", prev_tt[par])
                    elif bpar >= 2:
                        self.wait("sync", "pe", grp_pe_end[gi - 1])
                    evx = self.sync_dma(xtd_l[bpar][:],
                                        x_src[hg * 128:(hg + 1) * 128,
                                              c2 * 1024:(c2 + 1) * 1024], f"ldx{par}")
                    self.wait("gpsimd", f"ldx{par}", evx)
                    self.wait("gpsimd", "dve", res_gate[h])
                    if idx >= 2:
                        self.wait("gpsimd", f"stx{par}", st_half[idx - 2])
                    elif prev_st is not None:
                        self.wait("gpsimd", f"stx{par}", prev_st[par])
                    xo_ev[idx] = self.op("gpsimd", lambda e, h=h, c2=c2, par=par,
                                         bpar=bpar, xtd_l=xtd_l: e.tensor_tensor(
                        xod[par][:],
                        self.accA[:, h * 2048 + c2 * 1024: h * 2048 + (c2 + 1) * 1024],
                        xtd_l[bpar][:], ALU.add), inc=("plr", 1))
                    res_pl[half].append(xo_ev[idx])
                    self.wait("sync", "plr", xo_ev[idx])
                    st_half[idx] = self.sync_dma(
                        x_dst[hg * 128:(hg + 1) * 128, c2 * 1024:(c2 + 1) * 1024],
                        xod[par][:], f"stx{par}")
                    st_x[half * 16 + idx] = st_half[idx]
                    idx += 1
            self._res_prev_tt = (xo_ev[14], xo_ev[15])
            self._res_prev_st = (st_half[14], st_half[15])
        return st_x

    def emit_attn_b(self, x_src, x_dst, ev_z):
        C = self.C_u8
        zA, Bb = self.zA, self.Bb
        lgb = [self.cv(C, 0, 2048, F32), self.cv(C, 49152, 2048, F32)]
        exb = [self.cv(C, 8192, 2048, BF16), self.cv(C, 57344, 2048, BF16)]
        att = [self.cv(C, 12288 + i * 4096, 2048, BF16) for i in range(2)]
        attT = [self.cv(C, 20480 + i * 4096, 2048, BF16) for i in range(2)]
        nmxb = [self.cv(C, 28672 + 32 * i, 1, F32) for i in range(2)]
        ssumb = [self.cv(C, 28688 + 32 * i, 1, F32) for i in range(2)]
        xtb = [self.cv(C, 32768 + i * 4096, 1024, F32) for i in range(2)]
        xob = [self.cv(C, 40960 + i * 4096, 1024, F32) for i in range(2)]
        self.gate("scalar", "pe", "dve")
        self.wait("scalar", "plr", self.cnt["plr"])
        ev_z3 = None
        for j in range(KH):
            for half in range(2):
                bank = (1, half)
                self.wait("tensor", *self.z_evs[j])
                self.bank_w("tensor", [bank])
                pe_t = None
                for t in range(8):
                    tt = half * 8 + t
                    pe_t = self.op("tensor", lambda e, j=j, tt=tt, half=half, t=t:
                                   e.transpose(
                        self.psb(1, half)[:, t * 128:(t + 1) * 128],
                        zA[:, j * S + tt * 128: j * S + (tt + 1) * 128],
                        self.identb[:]), inc=("pe", 1) if t == 7 else None)
                self.bank_w_done([bank], "pe", pe_t)
                self.bank_r("scalar", [bank])
                dst = Bb[:].rearrange("p (t c) -> p t c", t=NT)[
                    :, half * 8:(half + 1) * 8, j * 128:(j + 1) * 128]
                src = self.psb(1, half)[:].rearrange("p (t c) -> p t c", t=8)
                ev_z3 = self.op("scalar", lambda e, dst=dst, src=src: e.copy(
                    dst, src), inc=("act", 1))
                self.bank_r_done([bank], "act", ev_z3)
        enb = [(0, s) for s in range(NS)]
        outb = [(1, 2), (1, 3)]
        st_ev = [None] * (KH * 2)
        ev_att = [None] * KH
        ev_at = [None] * KH
        ev_nmv = [None] * KH
        for it in range(KH + 1):
            i = it
            if i < KH:
                lg = lgb[i % 2]
                ex = exb[i % 2]
                nmx = nmxb[i % 2]
                ssum = ssumb[i % 2]
                self.wait("tensor", "act", ev_z3)
                self.bank_w("tensor", enb)
                pe_en = None
                for s in range(NS):
                    for t in range(NT):
                        pe_en = self.op("tensor", lambda e, i=i, s=s, t=t: e.matmul(
                            self.ps(0, s), Bb[:, t * S + i * 128:t * S + (i + 1) * 128],
                            Bb[:, t * S + s * 512: t * S + (s + 1) * 512],
                            start=(t == 0), stop=(t == NT - 1)),
                            inc=("pe", 1) if (s == NS - 1 and t == NT - 1) else None)
                self.bank_w_done(enb, "pe", pe_en)
                self.bank_r("scalar", enb)
                if i >= 2:
                    self.wait("scalar", "dve", ev_nmv[i - 2])
                ev_lg = self.op("scalar", lambda e, lg=lg: e.copy(lg[:], self.PS[0][:]),
                                inc=("act", 1))
                self.bank_r_done(enb, "act", ev_lg)
                if i >= 2:
                    self.wait("scalar", "dve", ev_att[i - 2])
                self.wait("vector", "act", ev_lg)
                self.op("vector", lambda e, nmx=nmx, lg=lg: e.reduce_max(
                    nmx[:], lg[:], axis=AX.X, negate=True))
                ev_nmv[i] = ev_nm = self.op("vector", lambda e: e.engine_nop(),
                                            inc=("dve", 1))
                self.wait("scalar", "dve", ev_nm)
                ev_ex = self.op("scalar", lambda e, ex=ex, lg=lg, nmx=nmx: e.activation(
                    ex[:], lg[:], AF.Exp, bias=nmx[:]), inc=("act", 1))
                self.wait("vector", "act", ev_ex)
                self.op("vector", lambda e, ssum=ssum, ex=ex: e.reduce_sum(
                    ssum[:], ex[:], axis=AX.X))
                self.fence_v()
                self.op("vector", lambda e, ssum=ssum: e.reciprocal(ssum[:], ssum[:]))
                self.fence_v()
                if i >= 2:
                    self.wait("vector", "pe", at_pe[i - 2])
                ev_att[i] = self.op("vector", lambda e, i=i, ex=ex, ssum=ssum:
                                    e.tensor_scalar_mul(
                    att[i % 2][:], ex[:], ssum[:]), inc=("dve", 1))
            j = it - 1
            if 0 <= j < KH:
                if j == 0:
                    at_pe = [None] * KH
                self.wait("tensor", "dve", ev_att[j])
                for half in range(2):
                    bank = (1, half)
                    self.bank_w("tensor", [bank])
                    pe_t = None
                    for t in range(8):
                        tt = half * 8 + t
                        pe_t = self.op("tensor", lambda e, j=j, tt=tt, half=half, t=t:
                                       e.transpose(
                            self.psb(1, half)[:, t * 128:(t + 1) * 128],
                            att[j % 2][:, tt * 128:(tt + 1) * 128], self.identb[:]),
                            inc=("pe", 1) if t == 7 else None)
                    at_pe[j] = pe_t
                    self.bank_w_done([bank], "pe", pe_t)
                    self.bank_r("vector", [bank])
                    ev_at[j] = self.op("vector", lambda e, j=j, half=half: e.tensor_copy(
                        attT[j % 2][:, half * 1024:(half + 1) * 1024],
                        self.psb(1, half)), inc=("dve", 1))
                    self.bank_r_done([bank], "dve", ev_at[j])
                for ch in range(2):
                    self.wait("tensor", "dve", ev_at[j])
                    self.bank_w("tensor", outb)
                    pe_o = None
                    for t in range(NT):
                        for s in range(2):
                            pe_o = self.op("tensor", lambda e, j=j, ch=ch, t=t, s=s:
                                           e.matmul(
                                self.PS[1][:, (2 + s) * 512:(3 + s) * 512],
                                attT[j % 2][:, t * 128:(t + 1) * 128],
                                zA[:, t * S + ch * 1024 + s * 512:
                                   t * S + ch * 1024 + (s + 1) * 512],
                                start=(t == 0), stop=(t == NT - 1)),
                                inc=("pe", 1) if (t == NT - 1 and s == 1) else None)
                    self.bank_w_done(outb, "pe", pe_o)
                    idx = j * 2 + ch
                    par = idx % 2
                    if idx == 0:
                        xo_prev = [None] * (KH * 2)
                        self.gate("sync", "pe", "dve")
                    if idx >= 2:
                        self.wait("sync", "dve", xo_prev[idx - 2])
                    evx = self.sync_dma(xtb[par][:],
                                        x_src[j * 128:(j + 1) * 128,
                                              ch * 1024:(ch + 1) * 1024], f"ldx{par}")
                    self.bank_r("vector", outb)
                    self.wait("vector", f"ldx{par}", evx)
                    if idx >= 2:
                        self.wait("vector", f"stx{par}", st_ev[idx - 2])
                    xo_prev[idx] = self.op("vector", lambda e, par=par: e.tensor_tensor(
                        xob[par][:], self.PS[1][:, 1024:2048], xtb[par][:], ALU.add),
                        inc=("dve", 1))
                    self.bank_r_done(outb, "dve", xo_prev[idx])
                    self.wait("sync", "dve", xo_prev[idx])
                    st_ev[idx] = self.sync_dma(
                        x_dst[j * 128:(j + 1) * 128, ch * 1024:(ch + 1) * 1024],
                        xob[par][:], f"stx{par}")
        return st_ev

    def dump_arenas(self):
        nc = self.nc
        self.dmp = {}
        for nm, t, cols in (("dumpA", self.A_u8, 65536), ("dumpB", self.B_u8, 65536),
                            ("dumpC", self.C_u8, 73728), ("dumpD", self.D_u8, 6656)):
            d = nc.dram_tensor(nm, [128, cols], U8, kind="ExternalOutput").ap()
            self.dmp[nm] = (d, t)
        self.sems["dmp"] = nc.alloc_semaphore("dmp")
        self.cnt["dmp"] = 0
        for eng in ("sync",):
            for s in ("pe", "dve", "act"):
                self.wait(eng, s, self.cnt[s])
            self.gate_stores(eng)
        ev = None
        for nm, (d, t) in self.dmp.items():
            ev = self.sync_dma(d, t[:], "dmp")
        self.wait("sync", "dmp", ev)

    def snap(self):
        return {s: self.cnt[s] for s in ("pe", "dve", "act")}

    def gate_vals(self, eng, pg):
        if pg:
            for s, v in pg.items():
                self.wait(eng, s, v)

    def emit_all(self, stop=None):
        if self.dev:
            for t in (self.A_u8, self.B_u8, self.C_u8, self.D_u8):
                n = t.ap().shape[1] if hasattr(t, "ap") else t[:].shape[1]
                h = n // 2
                self.op("vector", lambda e, t=t, h=h: e.memset(t[:, 0:h], 0))
                self.op("vector", lambda e, t=t, h=h, n=n: e.memset(t[:, h:n], 0))
            ev_ms = self.op("vector", lambda e: e.engine_nop(), inc=("dve", 1))
            self.wait("sync", "dve", ev_ms)
        self.load_consts()
        x = self.din["x"]
        pg = self.snap()
        ev_z = self.emit_ln(x, prefetch=lambda pg=pg: self.attn_pre(pg))
        if stop == "ln":
            return
        st_a = self.emit_attn_a(x, self.xr[0], ev_z, stop=stop, pg=pg)
        if stop is not None and stop != "attn_a":
            return
        if stop == "attn_a":
            return
        pg = self.snap()
        ev_z = self.emit_ln(self.xr[0], st_gates=[
            [("stx%d" % (t % 2), st_a[t])] for t in range(KH)],
            prefetch=lambda pg=pg: self.ffn_pre(pg))
        st_f = self.emit_ffn(self.xr[0], self.xr[1], ev_z, pg=pg)
        pg = self.snap()
        ev_z = self.emit_ln(self.xr[1], st_gates=[
            [("stx0", st_f[(t // 8) * 16 + 2 * (t % 8)]),
             ("stx1", st_f[(t // 8) * 16 + 2 * (t % 8) + 1])] for t in range(KH)])
        st_b = self.emit_attn_b(self.xr[1], self.xr[2], ev_z)
        pg = self.snap()
        ev_z = self.emit_ln(self.xr[2], st_gates=[
            [("stx0", st_b[2 * t]), ("stx1", st_b[2 * t + 1])] for t in range(KH)],
            prefetch=lambda pg=pg: self.ffn_pre(pg))
        self.emit_ffn(self.xr[2], self.y, ev_z, pg=pg)
        self.gate_stores("sync")

    def finalize(self):
        with self.nc.Block() as block:
            @block.tensor
            def _(e):
                for fn in self.streams["tensor"]:
                    fn(e)

            @block.vector
            def _(e):
                for fn in self.streams["vector"]:
                    fn(e)

            @block.scalar
            def _(e):
                for fn in self.streams["scalar"]:
                    fn(e)

            @block.gpsimd
            def _(e):
                for fn in self.streams["gpsimd"]:
                    fn(e)

            @block.sync
            def _(e):
                for fn in self.streams["sync"]:
                    fn(e)
        return self.nc


def host_prep(inputs):
    f32 = np.float32
    g = lambda k: np.asarray(inputs[k], f32)
    w = {}
    w["wq_s"] = _lhsT_stream(g("L0_attn_a_Wq"))
    w["wk_s"] = _lhsT_stream(g("L0_attn_a_Wk"))
    w["wv_s"] = _lhsT_stream(g("L0_attn_a_Wv"))
    w["wphi_s"] = _lhsT_stream(g("L0_attn_a_Wphi"))
    w["wo_s"] = _lhsT_stream(g("L0_attn_a_Wo"))
    w["w1_s"] = _lhsT_stream(np.ascontiguousarray(g("L0_ffn_W1").T))
    w["w3_s"] = _lhsT_stream(np.ascontiguousarray(g("L0_ffn_W3").T))
    w["w2t"] = np.ascontiguousarray(g("L0_ffn_W2").T)
    w["b1s_s"] = _lhsT_stream(np.ascontiguousarray(
        g("L0_ffn_B1").transpose(0, 2, 1).reshape(ER, FF)))
    w["b3s_s"] = _lhsT_stream(np.ascontiguousarray(
        g("L0_ffn_B3").transpose(0, 2, 1).reshape(ER, FF)))
    w["a1st"] = _lhsT_stream(np.ascontiguousarray(g("L0_ffn_A1").reshape(ER, D).T))
    w["a3st"] = _lhsT_stream(np.ascontiguousarray(g("L0_ffn_A3").reshape(ER, D).T))
    w["a2st"] = np.ascontiguousarray(g("L0_ffn_A2").reshape(ER, FF).T)
    w["b2s"] = np.ascontiguousarray(g("L0_ffn_B2").transpose(0, 2, 1).reshape(ER, D))
    w["wg_t"] = np.ascontiguousarray(g("L0_ffn_Wg").T)
    xin = g("L0_attn_a_hidden_states")
    xs = [np.ascontiguousarray(xin[c].T) for c in range(8)]
    return w, xs


def kernel(**inputs):
    mk = MK(dev_outputs=False)
    mk.emit_all()
    nc = mk.finalize()
    w, xs = host_prep(inputs)
    in_maps = [dict(w, x=xs[c]) for c in range(8)]
    res = run_bass_kernel_spmd(nc, in_maps, core_ids=list(range(8)))
    out = np.stack([np.ascontiguousarray(np.asarray(res.results[c]["y"]).T)
                    for c in range(8)])
    return out.astype(np.float32)


# version 08
# speedup vs baseline: 1.1887x; 1.1887x over previous
import numpy as np
import ml_dtypes
import concourse.bass as bass
import concourse.mybir as mybir
from concourse.bass_utils import run_bass_kernel_spmd

F32 = mybir.dt.float32
BF16 = mybir.dt.bfloat16
U8 = mybir.dt.uint8
AF = mybir.ActivationFunctionType
ALU = mybir.AluOpType
AX = mybir.AxisListType

D = 2048
S = 2048
NH, NKV, HD = 32, 8, 64
FF = 7168
NE, RK = 8, 32
ER = NE * RK
MF = FF // 128
KH = D // 128
NT = S // 128
NS = 4
EPS = 1e-5
G_A2 = 3
G_DN = 8

ENGINES = ("tensor", "vector", "scalar", "gpsimd", "sync")


def _lhsT_stream(W):
    Kd, Md = W.shape
    return np.ascontiguousarray(
        W.reshape(Kd // 128, 128, Md // 128, 128).transpose(2, 1, 0, 3))


def _rope_tables():
    inv_freq = 1.0 / (10000.0 ** (np.arange(0, HD, 2, dtype=np.float32) / HD))
    pos = np.arange(S, dtype=np.float32)
    freqs = pos[:, None] * inv_freq[None, :]
    emb = np.concatenate([freqs, freqs], axis=-1)
    cos = np.cos(emb).astype(np.float32)
    sin = np.sin(emb).astype(np.float32)
    sins = sin.copy()
    sins[:, :HD // 2] = -sins[:, :HD // 2]
    cos_f = np.ascontiguousarray(cos.reshape(NT, 128, HD))
    sins_f = np.ascontiguousarray(sins.reshape(NT, 128, HD))
    return (cos_f.astype(ml_dtypes.bfloat16), sins_f.astype(ml_dtypes.bfloat16),
            cos_f, sins_f)


class MK:
    def __init__(self, dev_outputs=False, detect_races=False):
        self.dev = dev_outputs
        self.nc = bass.Bass("TRN2", target_bir_lowering=False,
                            detect_race_conditions=detect_races)
        self.streams = {e: [] for e in ENGINES}
        self.sems = {}
        self.cnt = {}
        self.guar = {e: {} for e in ENGINES}
        self.bk_writer = {}
        self.bk_readers = {}
        for s in ("ld2", "ldz0", "ldz1", "ldz2", "ldz3", "wa0", "wa1",
                  "wu0", "wu1", "w2s", "rng", "msc", "ldx0", "ldx1",
                  "stx0", "stx1", "stc0", "stc1", "pe", "act", "dve", "plr", "plz"):
            self.sems[s] = self.nc.alloc_semaphore(s)
            self.cnt[s] = 0
        self._dram()
        self._consts()
        self._sbuf()

    def emit(self, eng, fn):
        self.streams[eng].append(fn)

    def wait(self, eng, sname, val):
        if val is None or val <= 0:
            return
        if self.guar[eng].get(sname, 0) >= val:
            return
        self.guar[eng][sname] = val
        h = self.sems[sname]
        self.emit(eng, lambda e, h=h, v=val: e.wait_ge(h, v))

    def op(self, eng, build, inc=None):
        ev = None
        pend = None
        if inc is not None:
            sname, by = inc
            self.cnt[sname] += by
            ev = self.cnt[sname]
            pend = (self.sems[sname], by)

        def fn(e, build=build, pend=pend):
            ins = build(e)
            if pend is not None:
                ins.then_inc(pend[0], pend[1])
        self.emit(eng, fn)
        return ev

    def fence_v(self):
        ev = self.op("vector", lambda e: e.engine_nop(), inc=("dve", 1))
        self.wait("vector", "dve", ev)

    def gate(self, eng, *sems):
        for s in sems:
            self.wait(eng, s, self.cnt[s])

    def bank_w(self, eng, banks):
        for b in banks:
            for sname, v in self.bk_readers.get(b, {}).items():
                self.wait(eng, sname, v)

    def bank_w_done(self, banks, sname, val):
        for b in banks:
            self.bk_writer[b] = (sname, val)
            self.bk_readers[b] = {}

    def bank_r(self, eng, banks):
        for b in banks:
            w = self.bk_writer.get(b)
            if w is not None:
                self.wait(eng, w[0], w[1])

    def bank_r_done(self, banks, sname, val):
        for b in banks:
            d = self.bk_readers.setdefault(b, {})
            d[sname] = max(d.get(sname, 0), val)

    def _dram(self):
        nc = self.nc
        self.din = {}
        self.din["x"] = nc.dram_tensor("x", [D, S], F32, kind="ExternalInput").ap()
        for nm, shp in [
            ("wq_s", [KH, 128, KH, 128]), ("wk_s", [4, 128, KH, 128]),
            ("wv_s", [4, 128, KH, 128]), ("wphi_s", [KH, 128, KH, 128]),
            ("wo_s", [KH, 128, KH, 128]),
            ("w1_s", [MF, 128, KH, 128]), ("w3_s", [MF, 128, KH, 128]),
            ("w2t", [FF, D]),
            ("b1s_s", [MF, 128, 2, 128]), ("b3s_s", [MF, 128, 2, 128]),
            ("a1st", [2, 128, KH, 128]), ("a3st", [2, 128, KH, 128]),
            ("a2st", [FF, ER]), ("b2s", [ER, D]), ("wg_t", [D, NE]),
        ]:
            self.din[nm] = nc.dram_tensor(nm, shp, F32, kind="ExternalInput").ap()
        self.y = nc.dram_tensor("y", [D, S], F32, kind="ExternalOutput").ap()
        rkind = "ExternalOutput" if self.dev else "Internal"
        self.xr = [nc.dram_tensor(f"xr{i}", [D, S], F32, kind=rkind).ap()
                   for i in (1, 2, 3)]
        self.x2c_s = nc.dram_tensor("x2c_s", [MF, 128, S], BF16, kind="Internal").ap()

    def _consts(self):
        nc = self.nc
        cos_t, sins_t, cos_f, sins_f = _rope_tables()
        self.c_identb = nc.inline_tensor(np.eye(128, dtype=ml_dtypes.bfloat16), "c_identb").ap()
        self.c_identf = nc.inline_tensor(np.eye(128, dtype=np.float32), "c_identf").ap()
        self.c_ones128 = nc.inline_tensor(np.ones((128, 1), dtype=ml_dtypes.bfloat16), "c_ones128").ap()
        self.c_ones1x = nc.inline_tensor(np.ones((1, 128), dtype=np.float32), "c_ones1x").ap()
        s8 = np.zeros((NE, ER), dtype=ml_dtypes.bfloat16)
        for e in range(NE):
            s8[e, e * RK:(e + 1) * RK] = 1.0
        self.c_s8 = nc.inline_tensor(s8, "c_s8").ap()
        self.c_cos = nc.inline_tensor(cos_t, "c_cos").ap()
        self.c_sins = nc.inline_tensor(sins_t, "c_sins").ap()
        self.c_cosF = nc.inline_tensor(cos_f, "c_cosF").ap()
        self.c_sinsF = nc.inline_tensor(sins_f, "c_sinsF").ap()

    def _sbuf(self):
        nc = self.nc
        self.A_u8 = nc.alloc_sbuf_tensor("A_u8", [128, 65536], U8)
        self.B_u8 = nc.alloc_sbuf_tensor("B_u8", [128, 65536], U8)
        self.C_u8 = nc.alloc_sbuf_tensor("C_u8", [128, 73728], U8)
        self.D_u8 = nc.alloc_sbuf_tensor("D_u8", [128, 6656], U8)
        self.zA = self.cv(self.A_u8, 0, 32768, BF16)
        self.accA = self.cv(self.A_u8, 0, 16384, F32)
        self.Bb = self.cv(self.B_u8, 0, 32768, BF16)
        d = self.D_u8
        self.identb = self.cv(d, 0, 128, BF16)
        self.identf = self.cv(d, 256, 128, F32)
        self.ones128 = self.cv(d, 768, 1, BF16)
        self.ones1x = self.cv(d, 800, 128, F32, rows=1)
        self.s8 = self.cv(d, 1312, ER, BF16, rows=NE)
        self.cosD = self.cv(d, 1824, NT * HD, BF16)
        self.sinsD = self.cv(d, 3872, NT * HD, BF16)
        self.PS = [nc.alloc_psum_tensor("psA", [128, 2048], F32),
                   nc.alloc_psum_tensor("psB", [128, 2048], F32)]

    @staticmethod
    def cv(arena, byte_off, cols, dtype, rows=128):
        esz = 4 if dtype == F32 else (2 if dtype == BF16 else 1)
        ap = arena[:, byte_off:byte_off + cols * esz].bitcast(dtype)
        if rows != 128:
            ap = ap[0:rows, :]
        return ap

    def ps(self, t, s):
        return self.PS[t][:, s * 512:(s + 1) * 512]

    def psb(self, t, s2):
        return self.PS[t][:].bitcast(BF16)[:, s2 * 1024:(s2 + 1) * 1024]

    def cast_dma(self, dst, src, sname):
        return self.op("gpsimd", lambda e, d=dst, s=src: e.dma_start(out=d, in_=s),
                       inc=(sname, 16))

    def sync_dma(self, dst, src, sname):
        return self.op("sync", lambda e, d=dst, s=src: e.dma_start(out=d, in_=s),
                       inc=(sname, 16))

    def wait_zall(self, eng):
        for sname, v in self._z_final:
            self.wait(eng, sname, v)

    def gate_stores(self, eng):
        for s in ("stx0", "stx1", "stc0", "stc1"):
            self.wait(eng, s, self.cnt[s])

    def load_consts(self):
        ev = None
        for dst, src in [(self.identb, self.c_identb), (self.identf, self.c_identf),
                         (self.ones128, self.c_ones128), (self.ones1x, self.c_ones1x),
                         (self.s8, self.c_s8)]:
            ev = self.sync_dma(dst, src, "ld2")
        ev = self.sync_dma(self.cosD[:].rearrange("p (t d) -> p t d", t=NT),
                           self.c_cos.transpose([1, 0, 2]), "ld2")
        ev = self.sync_dma(self.sinsD[:].rearrange("p (t d) -> p t d", t=NT),
                           self.c_sins.transpose([1, 0, 2]), "ld2")
        for eng in ENGINES:
            self.wait(eng, "ld2", ev)

    def attn_pre(self, pg):
        C = self.C_u8
        wbuf = [self.cv(C, i * 4096, 2048, BF16) for i in range(2)]
        self.gate_vals("gpsimd", pg)
        self._attn_pre = [self.cast_dma(
            wbuf[m % 2][:], self.din["wq_s"][m].rearrange("p k j -> p (k j)"),
            f"wa{m % 2}") for m in range(2)]

    def ffn_pre(self, pg):
        C = self.C_u8
        din = self.din
        w1b = [self.cv(C, i * 4096, 2048, BF16) for i in range(2)]
        w3b = [self.cv(C, 8192 + i * 4096, 2048, BF16) for i in range(2)]
        b1b = [self.cv(C, 16384 + i * 512, 256, BF16) for i in range(2)]
        b3b = [self.cv(C, 17408 + i * 512, 256, BF16) for i in range(2)]
        a2b = [self.cv(C, 18432 + i * 512, 256, BF16) for i in range(G_A2)]
        wgb = self.cv(C, 19968, 128, BF16)
        self.gate_vals("gpsimd", pg)
        ev_wg = self.cast_dma(wgb[:].rearrange("p (k c) -> p k c", k=KH),
                              din["wg_t"].rearrange("(k p) c -> p k c", p=128), "msc")
        evs = []
        for m in range(2):
            par = m
            self.cast_dma(w1b[par][:],
                          din["w1_s"][m].rearrange("p k j -> p (k j)"), f"wu{par}")
            self.cast_dma(w3b[par][:],
                          din["w3_s"][m].rearrange("p k j -> p (k j)"), f"wu{par}")
            self.cast_dma(b1b[par][:].rearrange("p (k j) -> p k j", k=2),
                          din["b1s_s"][m], f"wu{par}")
            self.cast_dma(b3b[par][:].rearrange("p (k j) -> p k j", k=2),
                          din["b3s_s"][m], f"wu{par}")
            evs.append(self.cast_dma(a2b[m][:],
                                     din["a2st"][m * 128:(m + 1) * 128, :], f"wu{par}"))
        self._ffn_pre = (ev_wg, evs)

    def emit_ln(self, x_src, st_gates=None, prefetch=None):
        C = self.C_u8
        xsq = [self.cv(C, 20480 + i * 4096, 2048, BF16) for i in range(2)]
        row1 = self.cv(C, 28672, 2048, F32, rows=1)
        row2 = self.cv(C, 36864, 2048, F32, rows=1)
        row3 = self.cv(C, 45056, 2048, F32, rows=1)
        Rbf = self.cv(C, 53248, 2048, F32)
        MRb = self.cv(C, 61440, 2048, F32)
        ztmp = self.cv(C, 28672, 2048, F32)
        zA = self.zA
        if st_gates is None:
            self.gate("gpsimd", "pe", "dve", "act")
            self.gate_stores("gpsimd")
        self.gate("scalar", "pe", "dve")
        self.gate_stores("scalar")
        self.gate("vector", "pe", "act")
        self.gate_stores("vector")
        ev_ld = []
        for k in range(KH):
            if st_gates is not None:
                for sname, v in st_gates[k]:
                    self.wait("gpsimd", sname, v)
            ev = self.cast_dma(zA[:, k * S:(k + 1) * S],
                               x_src[k * 128:(k + 1) * 128, :], f"ldz{k % 4}")
            ev_ld.append((f"ldz{k % 4}", ev))
        allb = [(t, s) for t in range(2) for s in range(NS)]
        self.bank_w("tensor", allb)
        ev_sq = [None] * KH
        pe_ev = [None] * KH
        for k in range(KH):
            self.wait("scalar", ev_ld[k][0], ev_ld[k][1])
            if k >= 2:
                self.wait("scalar", "pe", pe_ev[k - 2])
            ev_sq[k] = self.op("scalar", lambda e, k=k: e.activation(
                xsq[k % 2][:], zA[:, k * S:(k + 1) * S], AF.Square), inc=("act", 1))
            self.wait("tensor", ev_ld[k][0], ev_ld[k][1])
            for s in range(NS):
                self.op("tensor", lambda e, k=k, s=s: e.matmul(
                    self.ps(0, s)[0:1, :], self.ones128[:],
                    zA[:, k * S + s * 512: k * S + (s + 1) * 512],
                    start=(k == 0), stop=(k == KH - 1)))
            self.wait("tensor", "act", ev_sq[k])
            for s in range(NS):
                pe_ev[k] = self.op("tensor", lambda e, k=k, s=s: e.matmul(
                    self.ps(1, s)[0:1, :], self.ones128[:],
                    xsq[k % 2][:, s * 512:(s + 1) * 512],
                    start=(k == 0), stop=(k == KH - 1)),
                    inc=("pe", 1) if s == NS - 1 else None)
        self.bank_w_done(allb, "pe", pe_ev[KH - 1])
        self.bank_r("vector", allb)
        for s in range(NS):
            self.op("vector", lambda e, s=s: e.tensor_copy(
                row1[:, s * 512:(s + 1) * 512], self.ps(0, s)[0:1, :]))
            self.op("vector", lambda e, s=s: e.tensor_copy(
                row2[:, s * 512:(s + 1) * 512], self.ps(1, s)[0:1, :]))
        ev = self.op("vector", lambda e: e.tensor_scalar_mul(row1[:], row1[:], 1.0 / D),
                     inc=("dve", 1))
        self.bank_r_done(allb, "dve", ev)
        self.wait("vector", "dve", ev)
        self.op("vector", lambda e: e.tensor_tensor(row3[:], row1[:], row1[:], ALU.mult))
        self.fence_v()
        ev_var = self.op("vector", lambda e: e.scalar_tensor_tensor(
            row2[:], row2[:], 1.0 / D, row3[:], ALU.mult, ALU.subtract), inc=("dve", 1))
        self.wait("vector", "dve", ev_var)
        ev_var = self.op("vector", lambda e: e.tensor_scalar_add(row2[:], row2[:], EPS),
                         inc=("dve", 1))
        self.wait("scalar", "dve", ev_var)
        self.op("scalar", lambda e: e.activation(row3[:], row2[:], AF.Ln))
        ev_rstd = self.op("scalar", lambda e: e.activation(
            row2[:], row3[:], AF.Exp, scale=-0.5), inc=("act", 1))
        self.wait("vector", "act", ev_rstd)
        ev_mr = self.op("vector", lambda e: e.tensor_tensor(
            row1[:], row1[:], row2[:], ALU.mult), inc=("dve", 1))
        self.wait("vector", "dve", ev_mr)
        self.wait("tensor", "act", ev_rstd)
        self.wait("tensor", "dve", ev_mr)
        self.bank_w("tensor", allb)
        pe_b = None
        for s in range(NS):
            self.op("tensor", lambda e, s=s: e.matmul(
                self.ps(0, s), self.ones1x[:], row2[:, s * 512:(s + 1) * 512],
                start=True, stop=True))
            pe_b = self.op("tensor", lambda e, s=s: e.matmul(
                self.ps(1, s), self.ones1x[:], row1[:, s * 512:(s + 1) * 512],
                start=True, stop=True), inc=("pe", 1))
        self.bank_w_done(allb, "pe", pe_b)
        self.bank_r("scalar", allb)
        self.op("scalar", lambda e: e.copy(Rbf[:], self.PS[0][:]))
        ev_e = self.op("scalar", lambda e: e.copy(MRb[:], self.PS[1][:]), inc=("act", 1))
        self.bank_r_done(allb, "act", ev_e)
        self.wait("vector", "act", ev_e)
        if prefetch is not None:
            prefetch()
        self.wait("gpsimd", "act", ev_e)
        ztmp2 = self.cv(C, 36864, 2048, F32)
        N_DVE = 11
        z_evs = []
        ev_z = None
        ev_zp = None
        for k in range(KH):
            if k < N_DVE:
                self.op("vector", lambda e, k=k: e.tensor_tensor(
                    ztmp[:], zA[:, k * S:(k + 1) * S], Rbf[:], ALU.mult))
                ev_z = self.op("vector", lambda e, k=k: e.tensor_tensor(
                    zA[:, k * S:(k + 1) * S], ztmp[:], MRb[:], ALU.subtract),
                    inc=("dve", 1))
                z_evs.append(("dve", ev_z))
            else:
                self.op("gpsimd", lambda e, k=k: e.tensor_tensor(
                    ztmp2[:], zA[:, k * S:(k + 1) * S], Rbf[:], ALU.mult))
                ev_zp = self.op("gpsimd", lambda e, k=k: e.tensor_tensor(
                    zA[:, k * S:(k + 1) * S], ztmp2[:], MRb[:], ALU.subtract),
                    inc=("plz", 1))
                z_evs.append(("plz", ev_zp))
        self.z_evs = z_evs
        self._z_final = [("dve", ev_z), ("plz", ev_zp)]
        return ev_z

    def emit_attn_a(self, x_src, x_dst, ev_z, stop=None, pg=None):
        C = self.C_u8
        din = self.din
        zA, Bb = self.zA, self.Bb
        wbuf = [self.cv(C, i * 4096, 2048, BF16) for i in range(2)]
        qfm = [self.cv(C, 8192 + i * 4096, 2048, BF16) for i in range(2)]
        qtm = [self.cv(C, 16384 + i * 4096, 2048, BF16) for i in range(2)]
        qkp = [self.cv(C, 24576 + i * 4096, 2048, BF16) for i in range(2)]
        kk_tm = self.cv(C, 49152, 8192, BF16)
        kfmA32 = self.cv(C, 8192, 2048, F32)
        ktm32 = self.cv(C, 16384, 2048, F32)
        kap32 = self.cv(C, 24576, 2048, F32)
        ex4 = self.cv(C, 33792, 2048, BF16, rows=4)
        cosF = self.cv(C, 37888, 1024, F32)
        sinsF = self.cv(C, 41984, 1024, F32)
        vtmA = self.cv(C, 24576, 4096, BF16)
        vtmB = self.cv(C, 65536, 4096, BF16)
        qg_sb = self.cv(self.D_u8, 5920, 16, F32)
        qgl = self.cv(self.D_u8, 5984, 32, F32)
        al_tm = self.cv(C, 32768, 512, BF16)
        nmx4 = self.cv(self.D_u8, 6112, 1, F32, rows=4)
        ssum4 = self.cv(self.D_u8, 6128, 1, F32, rows=4)
        lg_sb = self.cv(C, 0, 2048, F32, rows=4)
        ak_sb = [self.cv(C, 8192 + i * 4096, 2048, BF16) for i in range(2)]
        outer_sb = self.cv(C, 16384, 1024, BF16)
        res_sb = [self.cv(C, 24576 + i * 4096, 2048, BF16) for i in range(2)]
        xt_sb = [self.cv(C, 32768 + i * 8192, 2048, F32) for i in range(2)]
        xo_sb = [self.cv(C, 49152 + i * 8192, 2048, F32) for i in range(2)]

        def vt_ap(t, col0, ncol):
            base = vtmA if t < 8 else vtmB
            tt = t % 8
            return base[:, tt * 512 + col0: tt * 512 + col0 + ncol]

        def proj(wkey, Mb, tm_handler, fm_dst, need_kappa, first=False):
            pre = self._attn_pre if (first and getattr(self, "_attn_pre", None)) \
                else None
            if pre is not None:
                pass
            elif first and pg is not None:
                self.gate_vals("gpsimd", pg)
            else:
                self.gate("gpsimd", "pe")
            pe_m = [None] * Mb
            act_ev = [None] * Mb
            tr1_pe = [None] * Mb
            tr2_pe = [None] * Mb
            stage1_ev = [None] * Mb
            ex_ev = [None] * Mb
            projb = [(0, s) for s in range(NS)]
            tr1b = [(1, 0), (1, 1)]
            tr2b = [(1, 2), (1, 3)]
            evw = [None] * Mb
            for it in range(Mb + 2):
                m = it
                if m < Mb:
                    par = m % 2
                    if pre is not None and m < 2:
                        evw[m] = pre[m]
                        if m == 1:
                            self._attn_pre = None
                    else:
                        if m >= 2:
                            self.wait("gpsimd", "pe", pe_m[m - 2])
                        evw[m] = self.cast_dma(
                            wbuf[par][:], din[wkey][m].rearrange("p k j -> p (k j)"),
                            f"wa{par}")
                    self.wait("tensor", f"wa{par}", evw[m])
                    self.bank_w("tensor", projb)
                    lastmm = None
                    for s in range(NS):
                        for k in range(KH):
                            self.wait("tensor", *self.z_evs[k])
                            lastmm = self.op("tensor", lambda e, s=s, k=k, par=par: e.matmul(
                                self.ps(0, s), wbuf[par][:, k * 128:(k + 1) * 128],
                                zA[:, k * S + s * 512: k * S + (s + 1) * 512],
                                start=(k == 0), stop=(k == KH - 1)),
                                inc=("pe", 1) if (s == NS - 1 and k == KH - 1) else None)
                    pe_m[m] = lastmm
                    self.bank_w_done(projb, "pe", pe_m[m])
                    self.bank_r("scalar", projb)
                    if m >= 2:
                        self.wait("scalar", "pe", tr1_pe[m - 2])
                    act_ev[m] = self.op("scalar", lambda e, m=m: e.copy(
                        qfm[m % 2][:], self.PS[0][:]), inc=("act", 1))
                    self.bank_r_done(projb, "act", act_ev[m])
                j = it - 1
                if 0 <= j < Mb:
                    self.wait("tensor", "act", act_ev[j])
                    self.bank_w("tensor", tr1b)
                    tp = None
                    for half in range(2):
                        for t in range(8):
                            tp = self.op("tensor", lambda e, j=j, half=half, t=t: e.transpose(
                                self.psb(1, half)[:, t * 128:(t + 1) * 128],
                                qfm[j % 2][:, half * 1024 + t * 128:
                                           half * 1024 + (t + 1) * 128],
                                self.identb[:]),
                                inc=("pe", 1) if (half == 1 and t == 7) else None)
                    tr1_pe[j] = tp
                    self.bank_w_done(tr1b, "pe", tr1_pe[j])
                    self.bank_r("vector", tr1b)
                    if j >= 2 and need_kappa:
                        self.wait("vector", "act", ex_ev[j - 2])
                    ev_tm = None
                    for half in range(2):
                        ev_tm = self.op("vector", lambda e, j=j, half=half: e.tensor_copy(
                            qtm[j % 2][:, half * 1024:(half + 1) * 1024],
                            self.psb(1, half)), inc=("dve", 1) if half == 1 else None)
                    self.bank_r_done(tr1b, "dve", ev_tm)
                    if need_kappa:
                        h2 = HD // 2
                        qv = qtm[j % 2][:].rearrange("p (t h d) -> p t h d", t=NT, h=2)
                        kv = qkp[j % 2][:].rearrange("p (t h d) -> p t h d", t=NT, h=2)
                        cosv = self.cosD[:].rearrange("p (t d) -> p t d", t=NT)
                        sinv = self.sinsD[:].rearrange("p (t d) -> p t d", t=NT)
                        if j >= 2:
                            self.wait("vector", "pe", tr2_pe[j - 2])
                        self.op("vector", lambda e, qv=qv, kv=kv, sinv=sinv, h2=h2:
                                e.tensor_tensor(kv[:, :, :, 0:h2], qv[:, :, :, h2:HD],
                                sinv[:, :, 0:h2].unsqueeze(2).broadcast_to([128, NT, 2, h2]),
                                ALU.mult))
                        self.op("vector", lambda e, qv=qv, kv=kv, sinv=sinv, h2=h2:
                                e.tensor_tensor(kv[:, :, :, h2:HD], qv[:, :, :, 0:h2],
                                sinv[:, :, h2:HD].unsqueeze(2).broadcast_to([128, NT, 2, h2]),
                                ALU.mult))
                        self.op("vector", lambda e, qv=qv, cosv=cosv: e.tensor_tensor(
                            qv, qv,
                            cosv.unsqueeze(2).broadcast_to([128, NT, 2, HD]), ALU.mult))
                        self.op("vector", lambda e, j=j: e.tensor_tensor(
                            qkp[j % 2][:], qtm[j % 2][:], qkp[j % 2][:], ALU.add))
                        ev_mn = self.op("vector", lambda e, j=j: e.tensor_scalar_min(
                            qtm[j % 2][:], qkp[j % 2][:], 0.0), inc=("dve", 1))
                        self.wait("scalar", "dve", ev_mn)
                        ex_ev[j] = self.op("scalar", lambda e, j=j: e.activation(
                            qtm[j % 2][:], qtm[j % 2][:], AF.Exp), inc=("act", 1))
                        self.op("vector", lambda e, j=j: e.tensor_scalar_max(
                            qkp[j % 2][:], qkp[j % 2][:], 0.0))
                        self.wait("vector", "act", ex_ev[j])
                        stage1_ev[j] = self.op("vector", lambda e, j=j: e.tensor_tensor(
                            qkp[j % 2][:], qtm[j % 2][:], qkp[j % 2][:], ALU.add),
                            inc=("dve", 1))
                        tm_handler(j, qkp[j % 2], stage1_ev[j])
                    else:
                        stage1_ev[j] = ev_tm
                        tm_handler(j, qtm[j % 2], stage1_ev[j])
                i = it - 2
                if 0 <= i < Mb and fm_dst is not None:
                    src = qkp[i % 2] if need_kappa else qtm[i % 2]
                    self.wait("tensor", "dve", stage1_ev[i])
                    self.bank_w("tensor", tr2b)
                    tp = None
                    for half in range(2):
                        for t in range(8):
                            tp = self.op("tensor", lambda e, half=half, t=t, src=src:
                                         e.transpose(
                                self.psb(1, 2 + half)[:, t * 128:(t + 1) * 128],
                                src[:, half * 1024 + t * 128: half * 1024 + (t + 1) * 128],
                                self.identb[:]),
                                inc=("pe", 1) if (half == 1 and t == 7) else None)
                    tr2_pe[i] = tp
                    self.bank_w_done(tr2b, "pe", tr2_pe[i])
                    self.bank_r("vector", tr2b)
                    ev2 = None
                    for half in range(2):
                        ev2 = self.op("vector", lambda e, i=i, half=half: e.tensor_copy(
                            fm_dst(i)[:, half * 1024:(half + 1) * 1024],
                            self.psb(1, 2 + half)), inc=("dve", 1) if half == 1 else None)
                    self.bank_r_done(tr2b, "dve", ev2)
                elif 0 <= i < Mb:
                    tr2_pe[i] = self.cnt["pe"]
            return

        proj("wq_s", KH, lambda j, buf, ev: None,
             lambda i: Bb[:, i * S:(i + 1) * S], True, first=True)
        if stop == "q":
            return None
        for mch in range(KH):
            self.op("vector", lambda e, mch=mch: e.reduce_sum(
                qg_sb[:, mch:mch + 1], Bb[:, mch * S:(mch + 1) * S], axis=AX.X))
        ev_qg = self.op("vector", lambda e: e.engine_nop(), inc=("dve", 1))
        for n in range(NH):
            mch, half = n // 2, n % 2
            self.op("vector", lambda e, n=n, mch=mch, half=half: e.tensor_scalar_mul(
                qgl[0:64, n:n + 1], qg_sb[half * 64:half * 64 + 64, mch:mch + 1], 1.0 / S))
            self.op("vector", lambda e, n=n, mch=mch, half=half: e.tensor_scalar_mul(
                qgl[64:128, n:n + 1], qg_sb[half * 64:half * 64 + 64, mch:mch + 1], 1.0 / S))
        ev_qgl = self.op("vector", lambda e: e.engine_nop(), inc=("dve", 1))

        self.gate("gpsimd", "pe")
        self.wait("sync", "dve", ev_z)
        ev_cf = self.sync_dma(cosF[:].rearrange("p (t d) -> p t d", t=NT),
                              self.c_cosF.transpose([1, 0, 2]), "ld2")
        ev_cf = self.sync_dma(sinsF[:].rearrange("p (t d) -> p t d", t=NT),
                              self.c_sinsF.transpose([1, 0, 2]), "ld2")
        projb = [(0, s) for s in range(NS)]
        trb4 = [(1, s) for s in range(NS)]
        cosv = cosF[:].rearrange("p (t d) -> p t d", t=NT)
        sinv = sinsF[:].rearrange("p (t d) -> p t d", t=NT)
        h2 = HD // 2
        pe_kp = [None] * 4
        for m in range(4):
            par = m % 2
            if m >= 2:
                self.wait("gpsimd", "pe", pe_kp[m - 2])
            evw = self.cast_dma(wbuf[par][:],
                                din["wk_s"][m].rearrange("p k j -> p (k j)"),
                                f"wa{par}")
            self.wait("tensor", f"wa{par}", evw)
            self.bank_w("tensor", projb)
            pe_p = None
            for s in range(NS):
                for k in range(KH):
                    self.wait("tensor", *self.z_evs[k])
                    pe_p = self.op("tensor", lambda e, s=s, k=k, par=par: e.matmul(
                        self.ps(0, s), wbuf[par][:, k * 128:(k + 1) * 128],
                        zA[:, k * S + s * 512: k * S + (s + 1) * 512],
                        start=(k == 0), stop=(k == KH - 1)),
                        inc=("pe", 1) if (s == NS - 1 and k == KH - 1) else None)
            pe_kp[m] = pe_p
            self.bank_w_done(projb, "pe", pe_p)
            self.bank_r("scalar", projb)
            ev_evac = self.op("scalar", lambda e: e.copy(kfmA32[:], self.PS[0][:]),
                              inc=("act", 1))
            self.bank_r_done(projb, "act", ev_evac)
            self.wait("tensor", "act", ev_evac)
            self.bank_w("tensor", trb4)
            pe_t1 = None
            for t in range(NT):
                pe_t1 = self.op("tensor", lambda e, t=t: e.transpose(
                    self.PS[1][:, (t // 4) * 512 + (t % 4) * 128:
                               (t // 4) * 512 + (t % 4) * 128 + 128],
                    kfmA32[:, t * 128:(t + 1) * 128], self.identf[:]),
                    inc=("pe", 1) if t == NT - 1 else None)
            self.bank_w_done(trb4, "pe", pe_t1)
            self.bank_r("vector", trb4)
            ev_tm = None
            for b in range(4):
                ev_tm = self.op("vector", lambda e, b=b: e.tensor_copy(
                    ktm32[:, b * 512:(b + 1) * 512], self.ps(1, b)),
                    inc=("dve", 1) if b == 3 else None)
            self.bank_r_done(trb4, "dve", ev_tm)
            if stop == "k1" and m == 3:
                return None
            self.wait("vector", "ld2", ev_cf)
            kv32 = kap32[:].rearrange("p (t h d) -> p t h d", t=NT, h=2)
            qv32 = ktm32[:].rearrange("p (t h d) -> p t h d", t=NT, h=2)
            self.op("vector", lambda e, qv32=qv32, kv32=kv32: e.tensor_tensor(
                kv32[:, :, :, 0:h2], qv32[:, :, :, h2:HD],
                sinv[:, :, 0:h2].unsqueeze(2).broadcast_to([128, NT, 2, h2]),
                ALU.mult))
            self.op("vector", lambda e, qv32=qv32, kv32=kv32: e.tensor_tensor(
                kv32[:, :, :, h2:HD], qv32[:, :, :, 0:h2],
                sinv[:, :, h2:HD].unsqueeze(2).broadcast_to([128, NT, 2, h2]),
                ALU.mult))
            self.op("vector", lambda e, qv32=qv32: e.tensor_tensor(
                qv32, qv32, cosv.unsqueeze(2).broadcast_to([128, NT, 2, HD]),
                ALU.mult))
            self.op("vector", lambda e: e.tensor_tensor(
                kap32[:], ktm32[:], kap32[:], ALU.add))
            if stop == "k2" and m == 3:
                self.op("vector", lambda e: e.engine_nop(), inc=("dve", 1))
                return None
            ev_mn = self.op("vector", lambda e: e.tensor_scalar_min(
                ktm32[:], kap32[:], 0.0), inc=("dve", 1))
            self.wait("scalar", "dve", ev_mn)
            ev_ex = self.op("scalar", lambda e: e.activation(
                ktm32[:], ktm32[:], AF.Exp), inc=("act", 1))
            self.op("vector", lambda e: e.tensor_scalar_max(
                kap32[:], kap32[:], 0.0))
            self.wait("vector", "act", ev_ex)
            ev_kap = self.op("vector", lambda e: e.tensor_tensor(
                kap32[:], ktm32[:], kap32[:], ALU.add), inc=("dve", 1))
            dstk = kk_tm[:].rearrange("p (t c) -> p t c", t=NT)[:, :, m * 128:(m + 1) * 128]
            srck = kap32[:].rearrange("p (t c) -> p t c", t=NT)
            self.op("vector", lambda e, dstk=dstk, srck=srck: e.tensor_copy(dstk, srck))
            self.wait("tensor", "dve", ev_kap)
            self.bank_w("tensor", trb4)
            pe_t2 = None
            for t in range(NT):
                pe_t2 = self.op("tensor", lambda e, t=t: e.transpose(
                    self.PS[1][:, (t // 4) * 512 + (t % 4) * 128:
                               (t // 4) * 512 + (t % 4) * 128 + 128],
                    kap32[:, t * 128:(t + 1) * 128], self.identf[:]),
                    inc=("pe", 1) if (t % 4) == 3 else None)
            self.bank_w_done(trb4, "pe", pe_t2)
            self.bank_r("vector", trb4)
            ev_fb = None
            for b in range(4):
                ev_fb = self.op("vector", lambda e, b=b: e.tensor_copy(
                    kap32[:, b * 512:(b + 1) * 512], self.ps(1, b)),
                    inc=("dve", 1))
            self.bank_r_done(trb4, "dve", ev_fb)
            self.wait("tensor", "dve", ev_fb)
            self.wait("tensor", "dve", ev_qgl)
            if m == 0:
                self._altr_pe = 0
            for hh in range(2):
                g = 2 * m + hh
                self.bank_w("tensor", projb)
                pe_lg = None
                for s in range(NS):
                    pe_lg = self.op("tensor", lambda e, g=g, hh=hh, s=s: e.matmul(
                        self.ps(0, s)[0:4, :],
                        qgl[hh * 64:hh * 64 + 64, 4 * g:4 * g + 4],
                        kap32[hh * 64:hh * 64 + 64, s * 512:(s + 1) * 512],
                        start=True, stop=True),
                        inc=("pe", 1) if s == NS - 1 else None)
                self.bank_w_done(projb, "pe", pe_lg)
                self.bank_r("vector", projb)
                self.op("vector", lambda e: e.reduce_max(
                    nmx4[:], self.PS[0][0:4, :], axis=AX.X, negate=True))
                ev_nm = self.op("vector", lambda e: e.engine_nop(), inc=("dve", 1))
                self.wait("scalar", "dve", ev_nm)
                self.wait("scalar", "pe", self._altr_pe)
                self.bank_r("scalar", projb)
                ev_exp = self.op("scalar", lambda e: e.activation(
                    ex4[:], self.PS[0][0:4, :], AF.Exp, bias=nmx4[:]),
                    inc=("act", 1))
                self.bank_r_done(projb, "act", ev_exp)
                self.bank_r_done(projb, "dve", ev_nm)
                self.wait("vector", "act", ev_exp)
                self.op("vector", lambda e: e.reduce_sum(ssum4[:], ex4[:], axis=AX.X))
                self.fence_v()
                self.op("vector", lambda e: e.reciprocal(ssum4[:], ssum4[:]))
                self.fence_v()
                ev_al = self.op("vector", lambda e: e.tensor_scalar(
                    ex4[:], ex4[:], ssum4[:], float(S), ALU.mult, ALU.mult),
                    inc=("dve", 1))
                self.wait("tensor", "dve", ev_al)
                for t in range(NT):
                    self.bank_w("tensor", [(1, 0)])
                    ev_tr = self.op("tensor", lambda e, t=t: e.transpose(
                        self.psb(1, 0)[0:128, 0:4],
                        ex4[:, t * 128:(t + 1) * 128], self.identb[0:4, 0:4]),
                        inc=("pe", 1))
                    self.bank_w_done([(1, 0)], "pe", ev_tr)
                    self.bank_r("vector", [(1, 0)])
                    ev_a2d = self.op("vector", lambda e, t=t, g=g: e.tensor_copy(
                        al_tm[:, t * 32 + 4 * g: t * 32 + 4 * g + 4],
                        self.psb(1, 0)[0:128, 0:4]), inc=("dve", 1))
                    self.bank_r_done([(1, 0)], "dve", ev_a2d)
                    self._altr_pe = ev_tr
        if stop == "k":
            return None

        def v_handler(j, buf, ev):
            src = buf[:].rearrange("p (t c) -> p t c", t=NT)
            dstA = vtmA[:].rearrange("p (t c) -> p t c", t=8)[:, :, j * 128:(j + 1) * 128]
            dstB = vtmB[:].rearrange("p (t c) -> p t c", t=8)[:, :, j * 128:(j + 1) * 128]
            self.op("vector", lambda e, dstA=dstA, src=src: e.tensor_copy(
                dstA, src[:, 0:8, :]))
            self.op("vector", lambda e, dstB=dstB, src=src: e.tensor_copy(
                dstB, src[:, 8:16, :]), inc=("dve", 1))

        proj("wv_s", 4, v_handler, None, False)
        if stop == "v":
            return None

        if stop == "a2":
            return None
        ob = [(1, 2), (1, 3)]
        self.gate("vector", "pe")
        self.bank_w("tensor", ob)
        ev_ak = [None] * NT
        pe_ak = [None] * NT
        for t in range(NT):
            kkr = kk_tm[:, t * 512:(t + 1) * 512].rearrange(
                "p (kv d) -> p kv d", kv=NKV).unsqueeze(2).broadcast_to([128, NKV, 4, HD])
            alr = al_tm[:, t * 32:(t + 1) * 32].rearrange(
                "p (kv r) -> p kv r", kv=NKV).unsqueeze(3).broadcast_to([128, NKV, 4, HD])
            akv = ak_sb[t % 2][:].rearrange("p (kv r d) -> p kv r d", kv=NKV, r=4)
            if t >= 2:
                self.wait("vector", "pe", pe_ak[t - 2])
            ev_ak[t] = self.op("vector", lambda e, akv=akv, kkr=kkr, alr=alr:
                               e.tensor_tensor(akv, kkr, alr, ALU.mult), inc=("dve", 1))
            self.wait("tensor", "dve", ev_ak[t])
            for n in range(NH):
                pair, base = n // 2, (n % 2) * 64
                kvh = n // 4
                st_f = (t == 0) and (n == 0 or n == 16)
                sp_f = (t == NT - 1) and (n == 15 or n == 31)
                pe_ak[t] = self.op("tensor", lambda e, t=t, n=n, pair=pair, base=base,
                                   kvh=kvh, st_f=st_f, sp_f=sp_f: e.matmul(
                    self.PS[1][base:base + 64, 1024 + pair * 64: 1024 + pair * 64 + 64],
                    ak_sb[t % 2][:, n * HD:(n + 1) * HD],
                    vt_ap(t, kvh * HD, HD),
                    start=st_f, stop=sp_f, skip_group_check=True),
                    inc=("pe", 1) if n == NH - 1 else None)
        self.bank_w_done(ob, "pe", pe_ak[NT - 1])
        self.bank_r("vector", ob)
        ev_outer = None
        for half in range(2):
            ev_outer = self.op("vector", lambda e, half=half: e.tensor_copy(
                outer_sb[:, half * 512:(half + 1) * 512],
                self.PS[1][:, 1024 + half * 512: 1024 + (half + 1) * 512]),
                inc=("dve", 1))
        self.bank_r_done(ob, "dve", ev_outer)

        if stop == "a3":
            return None
        self.gate("gpsimd", "pe", "dve", "act")
        resb = [(0, s) for s in range(NS)]
        phib = [(1, s) for s in range(NS)]
        evw = [None] * KH
        pe_phi = [None] * KH
        for m in range(KH):
            par = m % 2
            if m >= 2:
                self.wait("gpsimd", "pe", pe_phi[m - 2])
            evw[m] = self.cast_dma(wbuf[par][:],
                                   din["wphi_s"][m].rearrange("p k j -> p (k j)"),
                                   f"wa{par}")
            self.wait("tensor", "dve", ev_outer)
            self.bank_w("tensor", resb)
            pe_res = None
            for hh in range(2):
                n = 2 * m + hh
                base = hh * 64
                for s in range(NS):
                    pe_res = self.op("tensor", lambda e, m=m, n=n, base=base, s=s: e.matmul(
                        self.ps(0, s)[base:base + 64, :],
                        outer_sb[base:base + 64, m * 64:(m + 1) * 64],
                        Bb[base:base + 64, m * S + s * 512: m * S + (s + 1) * 512],
                        start=True, stop=True),
                        inc=("pe", 1) if (hh == 1 and s == NS - 1) else None)
            self.bank_w_done(resb, "pe", pe_res)
            self.bank_r("scalar", resb)
            if m >= 2:
                self.wait("scalar", "dve", ctx_ev[m - 2])
            ev_resev = self.op("scalar", lambda e, m=m: e.copy(
                res_sb[m % 2][:], self.PS[0][:]), inc=("act", 1))
            self.bank_r_done(resb, "act", ev_resev)
            self.wait("tensor", f"wa{par}", evw[m])
            self.bank_w("tensor", phib)
            for s in range(NS):
                for k in range(KH):
                    pe_phi[m] = self.op("tensor", lambda e, s=s, k=k, par=par: e.matmul(
                        self.ps(1, s), wbuf[par][:, k * 128:(k + 1) * 128],
                        zA[:, k * S + s * 512: k * S + (s + 1) * 512],
                        start=(k == 0), stop=(k == KH - 1)),
                        inc=("pe", 1) if (s == NS - 1 and k == KH - 1) else None)
            self.bank_w_done(phib, "pe", pe_phi[m])
            self.bank_r("vector", phib)
            self.wait("vector", "act", ev_resev)
            if m == 0:
                ctx_ev = [None] * KH
            ctx_ev[m] = self.op("vector", lambda e, m=m: e.tensor_tensor(
                Bb[:, m * S:(m + 1) * S], self.PS[1][:], res_sb[m % 2][:], ALU.mult),
                inc=("dve", 1))
            self.bank_r_done(phib, "dve", ctx_ev[m])
        ev_ctx_all = self.cnt["dve"]

        if stop == "a4":
            return None
        self.gate("sync", "pe", "dve")
        self.gate("gpsimd", "pe")
        wob = [(0, s) for s in range(NS)]
        evw = [None] * KH
        pe_wo = [None] * KH
        xo_ev = [None] * KH
        st_ev = [None] * KH
        for m in range(KH):
            par = m % 2
            if m >= 2:
                self.wait("gpsimd", "pe", pe_wo[m - 2])
            evw[m] = self.cast_dma(wbuf[par][:],
                                   din["wo_s"][m].rearrange("p k j -> p (k j)"),
                                   f"wa{par}")
            if m >= 2:
                self.wait("sync", "dve", xo_ev[m - 2])
            evx = self.sync_dma(xt_sb[par][:], x_src[m * 128:(m + 1) * 128, :],
                                f"ldx{par}")
            self.wait("tensor", f"wa{par}", evw[m])
            self.wait("tensor", "dve", ev_ctx_all)
            self.bank_w("tensor", wob)
            for s in range(NS):
                for k in range(KH):
                    pe_wo[m] = self.op("tensor", lambda e, s=s, k=k, par=par: e.matmul(
                        self.ps(0, s), wbuf[par][:, k * 128:(k + 1) * 128],
                        Bb[:, k * S + s * 512: k * S + (s + 1) * 512],
                        start=(k == 0), stop=(k == KH - 1)),
                        inc=("pe", 1) if (s == NS - 1 and k == KH - 1) else None)
            self.bank_w_done(wob, "pe", pe_wo[m])
            self.bank_r("vector", wob)
            self.wait("vector", f"ldx{par}", evx)
            if m >= 2:
                self.wait("vector", f"stx{par}", st_ev[m - 2])
            xo_ev[m] = self.op("vector", lambda e, par=par: e.tensor_tensor(
                xo_sb[par][:], self.PS[0][:], xt_sb[par][:], ALU.add), inc=("dve", 1))
            self.bank_r_done(wob, "dve", xo_ev[m])
            self.wait("sync", "dve", xo_ev[m])
            st_ev[m] = self.sync_dma(x_dst[m * 128:(m + 1) * 128, :], xo_sb[par][:],
                                     f"stx{par}")
        return st_ev

    def emit_ffn(self, x_src, x_dst, ev_z, pg=None):
        C = self.C_u8
        din = self.din
        zA, Bb = self.zA, self.Bb
        w1b = [self.cv(C, i * 4096, 2048, BF16) for i in range(2)]
        w3b = [self.cv(C, 8192 + i * 4096, 2048, BF16) for i in range(2)]
        b1b = [self.cv(C, 16384 + i * 512, 256, BF16) for i in range(2)]
        b3b = [self.cv(C, 17408 + i * 512, 256, BF16) for i in range(2)]
        a2b = [self.cv(C, 18432 + i * 512, 256, BF16) for i in range(G_A2)]
        wgb = self.cv(C, 19968, 128, BF16)
        x2st = [self.cv(C, 20480 + i * 4096, 2048, BF16) for i in range(2)]
        s1b = [self.cv(C, 28672 + i * 1024, 512, BF16) for i in range(2)]
        prob = self.cv(C, 30720, 128, F32)
        pmm = self.cv(C, 31232, 128, F32)
        m1c = self.cv(C, 31744, 128, F32)
        p1c = self.cv(C, 32256, 16, F32)
        p2c = self.cv(C, 32320, 16, F32)
        rw0 = self.cv(C, 32384, 16, F32)
        rw1 = self.cv(C, 32448, 16, F32)
        m1T = self.cv(C, 32768 if False else 32512, 2048, BF16, rows=NE)
        m2T = self.cv(C, 36608, 2048, BF16, rows=NE)
        rwrow = [self.cv(C, 40960 + i * 8192, 2048, F32, rows=1) for i in range(2)]
        mex = [self.cv(C, 40960 + i * 4096, 2048, BF16) for i in range(2)]
        strm = self.cv(C, 57344, 4096, BF16)
        a2acc = [self.cv(C, 40960 + i * 16384, 4096, F32) for i in range(2)]
        dnma2 = [self.cv(C, i * 4096, 2048, BF16) for i in range(4)]
        ma2mex = self.cv(C, 16384, 2048, BF16)
        b2sb = self.cv(C, 20480, 4096, BF16)
        w2g2 = [self.cv(C, 28672 + i * 16384, G_DN * 1024, BF16) for i in range(2)]
        xod = [self.cv(C, 16384, 1024, F32),
               self.cv(C, 61440, 1024, F32)]
        xtd = [self.cv(C, 65536 + i * 4096, 1024, F32) for i in range(2)]
        xtd2 = [self.cv(self.B_u8, i * 4096, 1024, F32) for i in range(2)]
        ring = [Bb[:, i * 2048:(i + 1) * 2048] for i in range(2 * G_A2)]
        off = 2 * G_A2
        ma1 = [Bb[:, (off + i) * 2048:(off + i + 1) * 2048] for i in range(2)]
        d1v = [Bb[:, (off + 2 + i) * 2048:(off + 3 + i) * 2048] for i in range(2)]
        ma3v = [Bb[:, (off + 4 + i) * 2048:(off + 5 + i) * 2048] for i in range(2)]
        d3v = [Bb[:, (off + 6 + i) * 2048:(off + 7 + i) * 2048] for i in range(2)]
        rwb = [Bb[:, (off + 8 + i) * 2048:(off + 9 + i) * 2048] for i in range(2)]
        dnring2 = [[Bb[:, (p * G_DN + i) * 2048:(p * G_DN + i + 1) * 2048]
                    for i in range(G_DN)] for p in range(2)]

        pre = getattr(self, "_ffn_pre", None)
        if pre is not None:
            ev_wg = pre[0]
        else:
            if pg is not None:
                self.gate_vals("gpsimd", pg)
            else:
                self.gate("gpsimd", "pe", "dve", "act")
                self.gate_stores("gpsimd")
            ev_wg = self.cast_dma(wgb[:].rearrange("p (k c) -> p k c", k=KH),
                                  din["wg_t"].rearrange("(k p) c -> p k c", p=128),
                                  "msc")
        gb0, gb1 = (1, 0), (1, 1)
        trb = [(1, 2)]
        self.wait("tensor", "msc", ev_wg)
        last_tr = None
        for t in range(NT):
            bsel = gb0 if t % 2 == 0 else gb1
            pcol = prob[:, t * 8:(t + 1) * 8]
            mcol = pmm[:, t * 8:(t + 1) * 8]
            ccol = m1c[:, t * 8:(t + 1) * 8]
            self.bank_w("tensor", [bsel])
            pe_g = None
            for k in range(KH):
                self.wait("tensor", *self.z_evs[k])
                pe_g = self.op("tensor", lambda e, t=t, k=k, bsel=bsel: e.matmul(
                    self.PS[bsel[0]][:, bsel[1] * 512: bsel[1] * 512 + 8],
                    zA[:, k * S + t * 128: k * S + (t + 1) * 128],
                    wgb[:, k * 8:(k + 1) * 8],
                    start=(k == 0), stop=(k == KH - 1)),
                    inc=("pe", 1) if k == KH - 1 else None)
            self.bank_w_done([bsel], "pe", pe_g)
            self.bank_r("vector", [bsel])
            self.op("vector", lambda e, t=t, bsel=bsel: e.reduce_max(
                p1c[:, t:t + 1], self.PS[bsel[0]][:, bsel[1] * 512: bsel[1] * 512 + 8],
                axis=AX.X, negate=True))
            ev0 = self.op("vector", lambda e: e.engine_nop(), inc=("dve", 1))
            self.wait("scalar", "dve", ev0)
            ev_e = self.op("scalar", lambda e, t=t, bsel=bsel, pcol=pcol: e.activation(
                pcol, self.PS[bsel[0]][:, bsel[1] * 512: bsel[1] * 512 + 8],
                AF.Exp, bias=p1c[:, t:t + 1]), inc=("act", 1))
            self.bank_r_done([bsel], "act", ev_e)
            self.wait("vector", "act", ev_e)
            self.op("vector", lambda e, t=t, pcol=pcol: e.reduce_sum(
                p2c[:, t:t + 1], pcol, axis=AX.X))
            self.fence_v()
            self.op("vector", lambda e, t=t: e.reciprocal(p2c[:, t:t + 1], p2c[:, t:t + 1]))
            self.fence_v()
            self.op("vector", lambda e, t=t, pcol=pcol: e.tensor_scalar_mul(
                pcol, pcol, p2c[:, t:t + 1]))
            self.fence_v()
            self.op("vector", lambda e, t=t, pcol=pcol: e.reduce_max(
                p1c[:, t:t + 1], pcol, axis=AX.X))
            self.fence_v()
            self.op("vector", lambda e, t=t, pcol=pcol, mcol=mcol: e.tensor_scalar(
                mcol, pcol, p1c[:, t:t + 1], None, ALU.is_equal))
            self.fence_v()
            self.op("vector", lambda e, mcol=mcol, ccol=ccol: e.tensor_scalar(
                ccol, mcol, -1.0, 1.0, ALU.mult, ALU.add))
            self.fence_v()
            self.op("vector", lambda e, pcol=pcol, ccol=ccol: e.tensor_tensor(
                ccol, pcol, ccol, ALU.mult))
            self.fence_v()
            self.op("vector", lambda e, t=t, ccol=ccol: e.reduce_max(
                p2c[:, t:t + 1], ccol, axis=AX.X))
            self.fence_v()
            self.op("vector", lambda e, t=t: e.tensor_tensor(
                rw0[:, t:t + 1], p1c[:, t:t + 1], p2c[:, t:t + 1], ALU.add))
            self.fence_v()
            self.op("vector", lambda e, t=t: e.reciprocal(rw0[:, t:t + 1], rw0[:, t:t + 1]))
            self.fence_v()
            self.op("vector", lambda e, t=t: e.tensor_tensor(
                rw1[:, t:t + 1], p2c[:, t:t + 1], rw0[:, t:t + 1], ALU.mult))
            self.fence_v()
            self.op("vector", lambda e, t=t: e.tensor_tensor(
                rw0[:, t:t + 1], p1c[:, t:t + 1], rw0[:, t:t + 1], ALU.mult))
            self.fence_v()
            ev_m = self.op("vector", lambda e, t=t, ccol=ccol: e.tensor_scalar(
                ccol, ccol, p2c[:, t:t + 1], None, ALU.is_equal), inc=("dve", 1))
            self.wait("tensor", "dve", ev_m)
            for src, dstT in ((mcol, m1T), (ccol, m2T)):
                self.bank_w("tensor", trb)
                ev_tr = self.op("tensor", lambda e, src=src: e.transpose(
                    self.ps(1, 2)[0:8, 0:128], src, self.identf[:]), inc=("pe", 1))
                self.bank_w_done(trb, "pe", ev_tr)
                self.bank_r("vector", trb)
                ev_cp = self.op("vector", lambda e, t=t, dstT=dstT: e.tensor_copy(
                    dstT[:, t * 128:(t + 1) * 128], self.ps(1, 2)[0:8, 0:128]),
                    inc=("dve", 1))
                self.bank_r_done(trb, "dve", ev_cp)
            for src, row in ((rw0, rwrow[0]), (rw1, rwrow[1])):
                self.bank_w("tensor", trb)
                ev_tr = self.op("tensor", lambda e, t=t, src=src: e.transpose(
                    self.ps(1, 2)[0:1, 0:128], src[:, t:t + 1], self.identf[:]),
                    inc=("pe", 1))
                self.bank_w_done(trb, "pe", ev_tr)
                self.bank_r("vector", trb)
                last_tr = self.op("vector", lambda e, t=t, row=row: e.tensor_copy(
                    row[:, t * 128:(t + 1) * 128], self.ps(1, 0)[0:1, 1024:1152]
                    if False else self.ps(1, 2)[0:1, 0:128]), inc=("dve", 1))
                self.bank_r_done(trb, "dve", last_tr)
        bb = [(0, s) for s in range(NS)]
        for i in range(2):
            self.wait("tensor", "dve", last_tr)
            self.bank_w("tensor", bb)
            pe_b = None
            for s in range(NS):
                pe_b = self.op("tensor", lambda e, i=i, s=s: e.matmul(
                    self.ps(0, s), self.ones1x[:], rwrow[i][:, s * 512:(s + 1) * 512],
                    start=True, stop=True), inc=("pe", 1) if s == NS - 1 else None)
            self.bank_w_done(bb, "pe", pe_b)
            self.bank_r("scalar", bb)
            ev_e = self.op("scalar", lambda e, i=i: e.copy(rwb[i][:], self.PS[0][:]),
                           inc=("act", 1))
            self.bank_r_done(bb, "act", ev_e)

        ab = [(0, s) for s in range(NS)]
        meb = [(1, s) for s in range(NS)]
        pe_a2g = {}

        def up_dmas(m):
            par = m % 2
            g = m // G_A2
            gs = m % G_A2
            self.cast_dma(w1b[par][:],
                          din["w1_s"][m].rearrange("p k j -> p (k j)"), f"wu{par}")
            self.cast_dma(w3b[par][:],
                          din["w3_s"][m].rearrange("p k j -> p (k j)"), f"wu{par}")
            self.cast_dma(b1b[par][:].rearrange("p (k j) -> p k j", k=2),
                          din["b1s_s"][m], f"wu{par}")
            self.cast_dma(b3b[par][:].rearrange("p (k j) -> p k j", k=2),
                          din["b3s_s"][m], f"wu{par}")
            if gs == 0 and g > 0:
                self.wait("gpsimd", "pe", pe_a2g[g - 1])
            return self.cast_dma(a2b[gs][:], din["a2st"][m * 128:(m + 1) * 128, :],
                                 f"wu{par}")
        if pre is not None:
            ev_pre = pre[1]
            self._ffn_pre = None
        else:
            ev_pre = [up_dmas(0), up_dmas(1)]
        self.wait_zall("gpsimd")
        self.gate_stores("gpsimd")
        ev_p2 = None
        for which, skey, madst, ddst in ((0, "a1st", ma1, d1v), (1, "a3st", ma3v, d3v)):
            ev_s = self.cast_dma(strm[:].rearrange("p (m k j) -> p m k j", m=2, k=KH),
                                 din[skey].transpose([1, 0, 2, 3]), "msc")
            for er in range(2):
                for mi, mT in ((0, m1T), (1, m2T)):
                    self.bank_w("tensor", meb)
                    pe_me = None
                    for s in range(NS):
                        pe_me = self.op("tensor", lambda e, er=er, s=s, mT=mT: e.matmul(
                            self.ps(1, s), self.s8[:, er * 128:(er + 1) * 128],
                            mT[:, s * 512:(s + 1) * 512], start=True, stop=True),
                            inc=("pe", 1) if s == NS - 1 else None)
                    self.bank_w_done(meb, "pe", pe_me)
                    self.bank_r("scalar", meb)
                    if not (which == 0 and er == 0 and mi == 0):
                        self.wait("scalar", "dve", ev_p2)
                    ev_me = self.op("scalar", lambda e, mi=mi: e.copy(
                        mex[mi][:], self.PS[1][:]), inc=("act", 1))
                    self.bank_r_done(meb, "act", ev_me)
                self.wait("tensor", "msc", ev_s)
                self.bank_w("tensor", ab)
                pe_a = None
                for s in range(NS):
                    for k in range(KH):
                        self.wait("tensor", *self.z_evs[k])
                        pe_a = self.op("tensor", lambda e, er=er, s=s, k=k: e.matmul(
                            self.ps(0, s),
                            strm[:, er * 2048 + k * 128: er * 2048 + (k + 1) * 128],
                            zA[:, k * S + s * 512: k * S + (s + 1) * 512],
                            start=(k == 0), stop=(k == KH - 1)),
                            inc=("pe", 1) if (s == NS - 1 and k == KH - 1) else None)
                self.bank_w_done(ab, "pe", pe_a)
                self.bank_r("vector", ab)
                self.wait("vector", "act", ev_me)
                self.op("vector", lambda e, madst=madst, er=er: e.tensor_tensor(
                    madst[er][:], self.PS[0][:], mex[0][:], ALU.mult))
                self.op("vector", lambda e, ddst=ddst, er=er: e.tensor_tensor(
                    ddst[er][:], self.PS[0][:], mex[1][:], ALU.mult))
                ev_p2 = self.op("vector", lambda e, madst=madst, ddst=ddst, er=er:
                                e.tensor_tensor(ddst[er][:], ddst[er][:], madst[er][:],
                                                ALU.subtract), inc=("dve", 1))
                self.bank_r_done(ab, "dve", ev_p2)

        x1b = lambda s: (0, s % 2)
        x3b = lambda s: (0, 2 + s % 2)
        a2bk = [(1, 2), (1, 3)]
        pe_up = [None] * MF
        st_x2c = [None] * MF
        self.wait("vector", "plr", self.cnt["plr"])
        dve_fold = [None] * MF
        for m in range(MF):
            par = m % 2
            g = m // G_A2
            gs = m % G_A2
            if m >= 2:
                self.wait("gpsimd", "pe", pe_up[m - 2])
                ev_grp = up_dmas(m)
            else:
                ev_grp = ev_pre[m]
            self.wait("tensor", f"wu{par}", ev_grp)
            if m == 0:
                self.wait_zall("tensor")
            if gs == 0 and g > 0:
                self.wait("vector", "pe", pe_a2g[g - 1])
            evacA = [None] * NS
            for s in range(NS):
                self.bank_w("tensor", [x1b(s)])
                pe1 = None
                for k in range(KH):
                    self.op("tensor", lambda e, s=s, k=k, par=par: e.matmul(
                        self.ps(*x1b(s)), w1b[par][:, k * 128:(k + 1) * 128],
                        zA[:, k * S + s * 512: k * S + (s + 1) * 512],
                        start=(k == 0), stop=False))
                for er in range(2):
                    pe1 = self.op("tensor", lambda e, s=s, er=er, par=par: e.matmul(
                        self.ps(*x1b(s)), b1b[par][:, er * 128:(er + 1) * 128],
                        ma1[er][:, s * 512:(s + 1) * 512],
                        start=False, stop=(er == 1)), inc=("pe", 1) if er == 1 else None)
                self.bank_w_done([x1b(s)], "pe", pe1)
                self.bank_r("scalar", [x1b(s)])
                evacA[s] = self.op("scalar", lambda e, s=s: e.activation(
                    s1b[s % 2][:], self.ps(*x1b(s)), AF.Silu), inc=("act", 1))
                self.bank_r_done([x1b(s)], "act", evacA[s])
            for s in range(NS):
                self.bank_w("tensor", [x3b(s)])
                pe3 = None
                for k in range(KH):
                    self.op("tensor", lambda e, s=s, k=k, par=par: e.matmul(
                        self.ps(*x3b(s)), w3b[par][:, k * 128:(k + 1) * 128],
                        zA[:, k * S + s * 512: k * S + (s + 1) * 512],
                        start=(k == 0), stop=False))
                for er in range(2):
                    pe3 = self.op("tensor", lambda e, s=s, er=er, par=par: e.matmul(
                        self.ps(*x3b(s)), b3b[par][:, er * 128:(er + 1) * 128],
                        ma3v[er][:, s * 512:(s + 1) * 512],
                        start=False, stop=(er == 1)), inc=("pe", 1) if er == 1 else None)
                self.bank_w_done([x3b(s)], "pe", pe3)
                self.bank_r("vector", [x3b(s)])
                self.wait("vector", "act", evacA[s])
                ev = self.op("vector", lambda e, s=s, gs=gs: e.tensor_tensor(
                    ring[2 * gs][:, s * 512:(s + 1) * 512], s1b[s % 2][:],
                    self.ps(*x3b(s)), ALU.mult), inc=("dve", 1))
                self.bank_r_done([x3b(s)], "dve", ev)
            evacA1 = [None] * NS
            for s in range(NS):
                self.bank_w("tensor", [x1b(s)])
                pe1 = None
                for er in range(2):
                    pe1 = self.op("tensor", lambda e, s=s, er=er, par=par: e.matmul(
                        self.ps(*x1b(s)), b1b[par][:, er * 128:(er + 1) * 128],
                        d1v[er][:, s * 512:(s + 1) * 512],
                        start=False, stop=(er == 1), skip_group_check=True),
                        inc=("pe", 1) if er == 1 else None)
                self.bank_w_done([x1b(s)], "pe", pe1)
                self.bank_r("scalar", [x1b(s)])
                evacA1[s] = self.op("scalar", lambda e, s=s: e.activation(
                    s1b[s % 2][:], self.ps(*x1b(s)), AF.Silu), inc=("act", 1))
                self.bank_r_done([x1b(s)], "act", evacA1[s])
            for s in range(NS):
                self.bank_w("tensor", [x3b(s)])
                pe3 = None
                for er in range(2):
                    pe3 = self.op("tensor", lambda e, s=s, er=er, par=par: e.matmul(
                        self.ps(*x3b(s)), b3b[par][:, er * 128:(er + 1) * 128],
                        d3v[er][:, s * 512:(s + 1) * 512],
                        start=False, stop=(er == 1), skip_group_check=True),
                        inc=("pe", 1) if er == 1 else None)
                self.bank_w_done([x3b(s)], "pe", pe3)
                self.bank_r("vector", [x3b(s)])
                self.wait("vector", "act", evacA1[s])
                ev = self.op("vector", lambda e, s=s, gs=gs: e.tensor_tensor(
                    ring[2 * gs + 1][:, s * 512:(s + 1) * 512], s1b[s % 2][:],
                    self.ps(*x3b(s)), ALU.mult), inc=("dve", 1))
                self.bank_r_done([x3b(s)], "dve", ev)
                sl = slice(s * 512, (s + 1) * 512)
                self.op("vector", lambda e, gs=gs, sl=sl: e.tensor_tensor(
                    ring[2 * gs][:, sl], ring[2 * gs][:, sl], rwb[0][:, sl], ALU.mult))
                ev_scale2 = self.op("vector", lambda e, gs=gs, sl=sl: e.tensor_tensor(
                    ring[2 * gs + 1][:, sl], ring[2 * gs + 1][:, sl], rwb[1][:, sl],
                    ALU.mult), inc=("dve", 1))
                if s == 0 and m >= 2:
                    self.wait("vector", f"stc{par}", st_x2c[m - 2])
                dve_fold[m] = self.op("vector", lambda e, gs=gs, par=par, sl=sl:
                                      e.tensor_tensor(
                    x2st[par][:, sl], ring[2 * gs][:, sl], ring[2 * gs + 1][:, sl],
                    ALU.add), inc=("dve", 1))
            self.wait("sync", "dve", dve_fold[m])
            st_x2c[m] = self.sync_dma(self.x2c_s[m], x2st[par][:], f"stc{par}")
            pe_up[m] = self.cnt["pe"]
            if gs == G_A2 - 1 or m == MF - 1:
                nmm = gs + 1
                self.wait("tensor", "dve", ev_scale2)
                for kk in range(2):
                    for er in range(2):
                        for s in range(NS):
                            bank = a2bk[(er * NS + s) % 2]
                            self.bank_w("tensor", [bank])
                            pe_t = None
                            for mi in range(nmm):
                                pe_t = self.op("tensor", lambda e, mi=mi, er=er, s=s,
                                               kk=kk, bank=bank, nmm=nmm: e.matmul(
                                    self.ps(*bank),
                                    a2b[mi][:, er * 128:(er + 1) * 128],
                                    ring[2 * mi + kk][:, s * 512:(s + 1) * 512],
                                    start=(mi == 0), stop=(mi == nmm - 1)),
                                    inc=("pe", 1) if mi == nmm - 1 else None)
                            self.bank_w_done([bank], "pe", pe_t)
                            self.bank_r("vector", [bank])
                            dst = a2acc[kk][:, er * 2048 + s * 512:
                                            er * 2048 + (s + 1) * 512]
                            tag = f"a2acc g{g} kk{kk} er{er} s{s} bank{bank}"
                            if g == 0:
                                ev_a = self.op("vector", lambda e, dst=dst, bank=bank,
                                               tag=tag: e.tensor_copy(
                                                   dst, self.ps(*bank)).annotate(tag),
                                               inc=("dve", 1))
                            else:
                                ev_a = self.op("vector", lambda e, dst=dst, bank=bank,
                                               tag=tag: e.tensor_tensor(
                                                   dst, dst, self.ps(*bank),
                                                   ALU.add).annotate(tag), inc=("dve", 1))
                            self.bank_r_done([bank], "dve", ev_a)
                pe_a2g[g] = self.cnt["pe"]
        ev_up_dve = self.cnt["dve"]
        pe_after_up = self.cnt["pe"]

        meb2 = [(1, s) for s in range(NS)]
        self.gate("gpsimd", "pe", "dve", "act")
        self.wait("vector", "plr", self.cnt["plr"])
        for kk in range(2):
            mT = m1T if kk == 0 else m2T
            for er in range(2):
                self.bank_w("tensor", meb2)
                pe_me = None
                for s in range(NS):
                    pe_me = self.op("tensor", lambda e, er=er, s=s, mT=mT: e.matmul(
                        self.ps(1, s), self.s8[:, er * 128:(er + 1) * 128],
                        mT[:, s * 512:(s + 1) * 512], start=True, stop=True),
                        inc=("pe", 1) if s == NS - 1 else None)
                self.bank_w_done(meb2, "pe", pe_me)
                self.bank_r("scalar", meb2)
                if not (kk == 0 and er == 0):
                    self.wait("scalar", "dve", ev_ma2)
                ev_mecp = self.op("scalar", lambda e: e.copy(ma2mex[:], self.PS[1][:]),
                                  inc=("act", 1))
                self.bank_r_done(meb2, "act", ev_mecp)
                self.wait("vector", "act", ev_mecp)
                self.wait("vector", "dve", ev_up_dve)
                ev_ma2 = self.op("vector", lambda e, kk=kk, er=er: e.tensor_tensor(
                    dnma2[kk * 2 + er][:], a2acc[kk][:, er * 2048:(er + 1) * 2048],
                    ma2mex[:], ALU.mult), inc=("dve", 1))
        ev_ma2_done = self.cnt["dve"]
        self.gate_stores("gpsimd")
        self.wait("gpsimd", "dve", ev_ma2_done)
        ev_b2 = self.cast_dma(b2sb[:].rearrange("p (k c) -> p k c", k=2),
                              din["b2s"].rearrange("(k p) c -> p k c", p=128), "msc")
        dnb = [(0, 0), (0, 1), (0, 2), (0, 3)]
        st_x = [None] * 32
        self.wait("sync", "pe", pe_after_up)
        self.wait("sync", "dve", ev_up_dve)
        self.gate_stores("sync")
        gplan = [(0, 4), (4, 4)] + [(8 + 8 * i, 8) for i in range((MF - 8) // 8)]
        groups = [(half, m0, nmm) for half in range(2) for m0, nmm in gplan]
        grp_pe_end = {}
        grp_ld = {}
        res_pl = {0: [], 1: []}
        res_gate = {}

        def load_rng(gi):
            half, m0, nmm = groups[gi]
            gp = gi % 2
            if gi >= 2:
                self.wait("sync", "pe", grp_pe_end[gi - 2])
            evr = None
            for mi in range(nmm):
                evr = self.sync_dma(dnring2[gp][mi][:], self.x2c_s[m0 + mi], "rng")
            return evr

        def load_w2(gi):
            half, m0, nmm = groups[gi]
            gp = gi % 2
            if gi >= 2:
                self.wait("gpsimd", "pe", grp_pe_end[gi - 2])
            evw2 = None
            for mi in range(nmm):
                m = m0 + mi
                evw2 = self.cast_dma(
                    w2g2[gp][:, mi * 1024:(mi + 1) * 1024],
                    din["w2t"][m * 128:(m + 1) * 128,
                               half * 1024:(half + 1) * 1024], "w2s")
            return evw2

        def load_group(gi):
            grp_ld[gi] = (load_rng(gi), load_w2(gi))

        load_group(0)
        load_group(1)
        ngh = len(gplan)
        for gi, (half, m0, nmm) in enumerate(groups):
            gp = gi % 2
            dnring = dnring2[gp]
            w2g = w2g2[gp]
            first_grp = (m0 == 0)
            last_grp = (gi % ngh == ngh - 1)
            self.wait("tensor", "w2s", grp_ld[gi][1])
            self.wait("tensor", "rng", grp_ld[gi][0])
            if last_grp:
                self.wait("tensor", "msc", ev_b2)
                self.wait("tensor", "dve", ev_ma2_done)
            for h in range(8):
                for s in range(NS):
                    bank = dnb[(h * NS + s) % 4]
                    self.bank_w("tensor", [bank])
                    pe_t = None
                    for mi in range(nmm):
                        lastmm = mi == nmm - 1
                        pe_t = self.op("tensor", lambda e, mi=mi, h=h, s=s, bank=bank,
                                nmm=nmm, lastmm=lastmm, w2g=w2g, dnring=dnring: e.matmul(
                            self.ps(*bank),
                            w2g[:, mi * 1024 + h * 128: mi * 1024 + (h + 1) * 128],
                            dnring[mi][:, s * 512:(s + 1) * 512],
                            start=(mi == 0), stop=lastmm),
                            inc=("pe", 1) if (lastmm and not last_grp) else None)
                    if last_grp:
                        hg = half * 8 + h
                        for kk in range(2):
                            for er in range(2):
                                lastmm = (kk == 1 and er == 1)
                                pe_t = self.op("tensor", lambda e, kk=kk, er=er, hg=hg,
                                        s=s, bank=bank, lastmm=lastmm: e.matmul(
                                    self.ps(*bank),
                                    b2sb[:, er * 2048 + hg * 128:
                                         er * 2048 + (hg + 1) * 128],
                                    dnma2[kk * 2 + er][:, s * 512:(s + 1) * 512],
                                    start=False, stop=lastmm,
                                    skip_group_check=True),
                                    inc=("pe", 1) if lastmm else None)
                    self.bank_w_done([bank], "pe", pe_t)
                    self.bank_r("vector", [bank])
                    dst = self.accA[:, h * 2048 + s * 512: h * 2048 + (s + 1) * 512]
                    if first_grp:
                        if half == 1:
                            self.wait("vector", "plr", res_pl[0][2 * h + 1])
                        ev_a = self.op("vector", lambda e, dst=dst, bank=bank:
                                       e.tensor_copy(dst, self.ps(*bank)),
                                       inc=("dve", 1))
                    else:
                        ev_a = self.op("vector", lambda e, dst=dst, bank=bank:
                                       e.tensor_tensor(dst, dst, self.ps(*bank),
                                                       ALU.add), inc=("dve", 1))
                    self.bank_r_done([bank], "dve", ev_a)
                    if last_grp and s == NS - 1:
                        res_gate[h] = ev_a
            grp_pe_end[gi] = self.cnt["pe"]
            if not last_grp:
                if gi + 2 < len(groups):
                    load_group(gi + 2)
                continue
            prev_tt = getattr(self, "_res_prev_tt", None)
            prev_st = getattr(self, "_res_prev_st", None)
            nbuf = 2 if half == 0 else 4
            xtd_l = xtd if half == 0 else [xtd[0], xtd[1], xtd2[0], xtd2[1]]
            xo_ev = {}
            st_half = [None] * 16
            evx = [None] * 16

            def issue_ldx(j):
                par = j % 2
                bpar = j % nbuf
                if j >= nbuf:
                    self.wait("sync", "plr", xo_ev[j - nbuf])
                elif bpar < 2 and prev_tt is not None:
                    self.wait("sync", "plr", prev_tt[par])
                elif bpar >= 2:
                    self.wait("sync", "pe", grp_pe_end[gi - 1])
                evx[j] = self.sync_dma(xtd_l[bpar][:],
                                       x_src[(half * 8 + j // 2) * 128:
                                             (half * 8 + j // 2 + 1) * 128,
                                             (j % 2) * 1024:(j % 2 + 1) * 1024],
                                       f"ldx{par}")

            for j in range(nbuf):
                issue_ldx(j)
            ev_rng2 = load_rng(gi + 2) if gi + 2 < len(groups) else None
            idx = 0
            for h in range(8):
                hg = half * 8 + h
                for c2 in range(2):
                    par = idx % 2
                    bpar = idx % nbuf
                    self.wait("gpsimd", f"ldx{par}", evx[idx])
                    self.wait("gpsimd", "dve", res_gate[h])
                    if idx >= 2:
                        self.wait("gpsimd", f"stx{par}", st_half[idx - 2])
                    elif prev_st is not None:
                        self.wait("gpsimd", f"stx{par}", prev_st[par])
                    xo_ev[idx] = self.op("gpsimd", lambda e, h=h, c2=c2, par=par,
                                         bpar=bpar, xtd_l=xtd_l: e.tensor_tensor(
                        xod[par][:],
                        self.accA[:, h * 2048 + c2 * 1024: h * 2048 + (c2 + 1) * 1024],
                        xtd_l[bpar][:], ALU.add), inc=("plr", 1))
                    res_pl[half].append(xo_ev[idx])
                    self.wait("sync", "plr", xo_ev[idx])
                    st_half[idx] = self.sync_dma(
                        x_dst[hg * 128:(hg + 1) * 128, c2 * 1024:(c2 + 1) * 1024],
                        xod[par][:], f"stx{par}")
                    st_x[half * 16 + idx] = st_half[idx]
                    if idx + nbuf < 16:
                        issue_ldx(idx + nbuf)
                    idx += 1
            if gi + 2 < len(groups):
                grp_ld[gi + 2] = (ev_rng2, load_w2(gi + 2))
            self._res_prev_tt = (xo_ev[14], xo_ev[15])
            self._res_prev_st = (st_half[14], st_half[15])
        return st_x

    def emit_attn_b(self, x_src, x_dst, ev_z):
        C = self.C_u8
        zA, Bb = self.zA, self.Bb
        lgb = [self.cv(C, 0, 2048, F32), self.cv(C, 49152, 2048, F32)]
        exb = [self.cv(C, 8192, 2048, BF16), self.cv(C, 57344, 2048, BF16)]
        att = [self.cv(C, 12288 + i * 4096, 2048, BF16) for i in range(2)]
        attT = [self.cv(C, 20480 + i * 4096, 2048, BF16) for i in range(2)]
        nmxb = [self.cv(C, 28672 + 32 * i, 1, F32) for i in range(2)]
        ssumb = [self.cv(C, 28688 + 32 * i, 1, F32) for i in range(2)]
        xtb = [self.cv(C, 32768 + i * 4096, 1024, F32) for i in range(2)]
        xob = [self.cv(C, 40960 + i * 4096, 1024, F32) for i in range(2)]
        self.gate("scalar", "pe", "dve")
        self.wait("scalar", "plr", self.cnt["plr"])
        ev_z3 = None
        for j in range(KH):
            for half in range(2):
                bank = (1, half)
                self.wait("tensor", *self.z_evs[j])
                self.bank_w("tensor", [bank])
                pe_t = None
                for t in range(8):
                    tt = half * 8 + t
                    pe_t = self.op("tensor", lambda e, j=j, tt=tt, half=half, t=t:
                                   e.transpose(
                        self.psb(1, half)[:, t * 128:(t + 1) * 128],
                        zA[:, j * S + tt * 128: j * S + (tt + 1) * 128],
                        self.identb[:]), inc=("pe", 1) if t == 7 else None)
                self.bank_w_done([bank], "pe", pe_t)
                self.bank_r("scalar", [bank])
                dst = Bb[:].rearrange("p (t c) -> p t c", t=NT)[
                    :, half * 8:(half + 1) * 8, j * 128:(j + 1) * 128]
                src = self.psb(1, half)[:].rearrange("p (t c) -> p t c", t=8)
                ev_z3 = self.op("scalar", lambda e, dst=dst, src=src: e.copy(
                    dst, src), inc=("act", 1))
                self.bank_r_done([bank], "act", ev_z3)
        enb = [(0, s) for s in range(NS)]
        outb = [(1, 2), (1, 3)]
        st_ev = [None] * (KH * 2)
        ev_att = [None] * KH
        ev_at = [None] * KH
        ev_nmv = [None] * KH
        for it in range(KH + 1):
            i = it
            if i < KH:
                lg = lgb[i % 2]
                ex = exb[i % 2]
                nmx = nmxb[i % 2]
                ssum = ssumb[i % 2]
                self.wait("tensor", "act", ev_z3)
                self.bank_w("tensor", enb)
                pe_en = None
                for s in range(NS):
                    for t in range(NT):
                        pe_en = self.op("tensor", lambda e, i=i, s=s, t=t: e.matmul(
                            self.ps(0, s), Bb[:, t * S + i * 128:t * S + (i + 1) * 128],
                            Bb[:, t * S + s * 512: t * S + (s + 1) * 512],
                            start=(t == 0), stop=(t == NT - 1)),
                            inc=("pe", 1) if (s == NS - 1 and t == NT - 1) else None)
                self.bank_w_done(enb, "pe", pe_en)
                self.bank_r("scalar", enb)
                if i >= 2:
                    self.wait("scalar", "dve", ev_nmv[i - 2])
                ev_lg = self.op("scalar", lambda e, lg=lg: e.copy(lg[:], self.PS[0][:]),
                                inc=("act", 1))
                self.bank_r_done(enb, "act", ev_lg)
                if i >= 2:
                    self.wait("scalar", "dve", ev_att[i - 2])
                self.wait("vector", "act", ev_lg)
                self.op("vector", lambda e, nmx=nmx, lg=lg: e.reduce_max(
                    nmx[:], lg[:], axis=AX.X, negate=True))
                ev_nmv[i] = ev_nm = self.op("vector", lambda e: e.engine_nop(),
                                            inc=("dve", 1))
                self.wait("scalar", "dve", ev_nm)
                ev_ex = self.op("scalar", lambda e, ex=ex, lg=lg, nmx=nmx: e.activation(
                    ex[:], lg[:], AF.Exp, bias=nmx[:]), inc=("act", 1))
                self.wait("vector", "act", ev_ex)
                self.op("vector", lambda e, ssum=ssum, ex=ex: e.reduce_sum(
                    ssum[:], ex[:], axis=AX.X))
                self.fence_v()
                self.op("vector", lambda e, ssum=ssum: e.reciprocal(ssum[:], ssum[:]))
                self.fence_v()
                if i >= 2:
                    self.wait("vector", "pe", at_pe[i - 2])
                ev_att[i] = self.op("vector", lambda e, i=i, ex=ex, ssum=ssum:
                                    e.tensor_scalar_mul(
                    att[i % 2][:], ex[:], ssum[:]), inc=("dve", 1))
            j = it - 1
            if 0 <= j < KH:
                if j == 0:
                    at_pe = [None] * KH
                self.wait("tensor", "dve", ev_att[j])
                for half in range(2):
                    bank = (1, half)
                    self.bank_w("tensor", [bank])
                    pe_t = None
                    for t in range(8):
                        tt = half * 8 + t
                        pe_t = self.op("tensor", lambda e, j=j, tt=tt, half=half, t=t:
                                       e.transpose(
                            self.psb(1, half)[:, t * 128:(t + 1) * 128],
                            att[j % 2][:, tt * 128:(tt + 1) * 128], self.identb[:]),
                            inc=("pe", 1) if t == 7 else None)
                    at_pe[j] = pe_t
                    self.bank_w_done([bank], "pe", pe_t)
                    self.bank_r("vector", [bank])
                    ev_at[j] = self.op("vector", lambda e, j=j, half=half: e.tensor_copy(
                        attT[j % 2][:, half * 1024:(half + 1) * 1024],
                        self.psb(1, half)), inc=("dve", 1))
                    self.bank_r_done([bank], "dve", ev_at[j])
                for ch in range(2):
                    self.wait("tensor", "dve", ev_at[j])
                    self.bank_w("tensor", outb)
                    pe_o = None
                    for t in range(NT):
                        for s in range(2):
                            pe_o = self.op("tensor", lambda e, j=j, ch=ch, t=t, s=s:
                                           e.matmul(
                                self.PS[1][:, (2 + s) * 512:(3 + s) * 512],
                                attT[j % 2][:, t * 128:(t + 1) * 128],
                                zA[:, t * S + ch * 1024 + s * 512:
                                   t * S + ch * 1024 + (s + 1) * 512],
                                start=(t == 0), stop=(t == NT - 1)),
                                inc=("pe", 1) if (t == NT - 1 and s == 1) else None)
                    self.bank_w_done(outb, "pe", pe_o)
                    idx = j * 2 + ch
                    par = idx % 2
                    if idx == 0:
                        xo_prev = [None] * (KH * 2)
                        self.gate("sync", "pe", "dve")
                    if idx >= 2:
                        self.wait("sync", "dve", xo_prev[idx - 2])
                    evx = self.sync_dma(xtb[par][:],
                                        x_src[j * 128:(j + 1) * 128,
                                              ch * 1024:(ch + 1) * 1024], f"ldx{par}")
                    self.bank_r("vector", outb)
                    self.wait("vector", f"ldx{par}", evx)
                    if idx >= 2:
                        self.wait("vector", f"stx{par}", st_ev[idx - 2])
                    xo_prev[idx] = self.op("vector", lambda e, par=par: e.tensor_tensor(
                        xob[par][:], self.PS[1][:, 1024:2048], xtb[par][:], ALU.add),
                        inc=("dve", 1))
                    self.bank_r_done(outb, "dve", xo_prev[idx])
                    self.wait("sync", "dve", xo_prev[idx])
                    st_ev[idx] = self.sync_dma(
                        x_dst[j * 128:(j + 1) * 128, ch * 1024:(ch + 1) * 1024],
                        xob[par][:], f"stx{par}")
        return st_ev

    def dump_arenas(self):
        nc = self.nc
        self.dmp = {}
        for nm, t, cols in (("dumpA", self.A_u8, 65536), ("dumpB", self.B_u8, 65536),
                            ("dumpC", self.C_u8, 73728), ("dumpD", self.D_u8, 6656)):
            d = nc.dram_tensor(nm, [128, cols], U8, kind="ExternalOutput").ap()
            self.dmp[nm] = (d, t)
        self.sems["dmp"] = nc.alloc_semaphore("dmp")
        self.cnt["dmp"] = 0
        for eng in ("sync",):
            for s in ("pe", "dve", "act"):
                self.wait(eng, s, self.cnt[s])
            self.gate_stores(eng)
        ev = None
        for nm, (d, t) in self.dmp.items():
            ev = self.sync_dma(d, t[:], "dmp")
        self.wait("sync", "dmp", ev)

    def snap(self):
        return {s: self.cnt[s] for s in ("pe", "dve", "act")}

    def gate_vals(self, eng, pg):
        if pg:
            for s, v in pg.items():
                self.wait(eng, s, v)

    def emit_all(self, stop=None):
        if self.dev:
            for t in (self.A_u8, self.B_u8, self.C_u8, self.D_u8):
                n = t.ap().shape[1] if hasattr(t, "ap") else t[:].shape[1]
                h = n // 2
                self.op("vector", lambda e, t=t, h=h: e.memset(t[:, 0:h], 0))
                self.op("vector", lambda e, t=t, h=h, n=n: e.memset(t[:, h:n], 0))
            ev_ms = self.op("vector", lambda e: e.engine_nop(), inc=("dve", 1))
            self.wait("sync", "dve", ev_ms)
        self.load_consts()
        x = self.din["x"]
        pg = self.snap()
        ev_z = self.emit_ln(x, prefetch=lambda pg=pg: self.attn_pre(pg))
        if stop == "ln":
            return
        st_a = self.emit_attn_a(x, self.xr[0], ev_z, stop=stop, pg=pg)
        if stop is not None and stop != "attn_a":
            return
        if stop == "attn_a":
            return
        pg = self.snap()
        ev_z = self.emit_ln(self.xr[0], st_gates=[
            [("stx%d" % (t % 2), st_a[t])] for t in range(KH)],
            prefetch=lambda pg=pg: self.ffn_pre(pg))
        st_f = self.emit_ffn(self.xr[0], self.xr[1], ev_z, pg=pg)
        pg = self.snap()
        ev_z = self.emit_ln(self.xr[1], st_gates=[
            [("stx0", st_f[(t // 8) * 16 + 2 * (t % 8)]),
             ("stx1", st_f[(t // 8) * 16 + 2 * (t % 8) + 1])] for t in range(KH)])
        st_b = self.emit_attn_b(self.xr[1], self.xr[2], ev_z)
        pg = self.snap()
        ev_z = self.emit_ln(self.xr[2], st_gates=[
            [("stx0", st_b[2 * t]), ("stx1", st_b[2 * t + 1])] for t in range(KH)],
            prefetch=lambda pg=pg: self.ffn_pre(pg))
        self.emit_ffn(self.xr[2], self.y, ev_z, pg=pg)
        self.gate_stores("sync")

    def finalize(self):
        with self.nc.Block() as block:
            @block.tensor
            def _(e):
                for fn in self.streams["tensor"]:
                    fn(e)

            @block.vector
            def _(e):
                for fn in self.streams["vector"]:
                    fn(e)

            @block.scalar
            def _(e):
                for fn in self.streams["scalar"]:
                    fn(e)

            @block.gpsimd
            def _(e):
                for fn in self.streams["gpsimd"]:
                    fn(e)

            @block.sync
            def _(e):
                for fn in self.streams["sync"]:
                    fn(e)
        return self.nc


def host_prep(inputs):
    f32 = np.float32
    g = lambda k: np.asarray(inputs[k], f32)
    w = {}
    w["wq_s"] = _lhsT_stream(g("L0_attn_a_Wq"))
    w["wk_s"] = _lhsT_stream(g("L0_attn_a_Wk"))
    w["wv_s"] = _lhsT_stream(g("L0_attn_a_Wv"))
    w["wphi_s"] = _lhsT_stream(g("L0_attn_a_Wphi"))
    w["wo_s"] = _lhsT_stream(g("L0_attn_a_Wo"))
    w["w1_s"] = _lhsT_stream(np.ascontiguousarray(g("L0_ffn_W1").T))
    w["w3_s"] = _lhsT_stream(np.ascontiguousarray(g("L0_ffn_W3").T))
    w["w2t"] = np.ascontiguousarray(g("L0_ffn_W2").T)
    w["b1s_s"] = _lhsT_stream(np.ascontiguousarray(
        g("L0_ffn_B1").transpose(0, 2, 1).reshape(ER, FF)))
    w["b3s_s"] = _lhsT_stream(np.ascontiguousarray(
        g("L0_ffn_B3").transpose(0, 2, 1).reshape(ER, FF)))
    w["a1st"] = _lhsT_stream(np.ascontiguousarray(g("L0_ffn_A1").reshape(ER, D).T))
    w["a3st"] = _lhsT_stream(np.ascontiguousarray(g("L0_ffn_A3").reshape(ER, D).T))
    w["a2st"] = np.ascontiguousarray(g("L0_ffn_A2").reshape(ER, FF).T)
    w["b2s"] = np.ascontiguousarray(g("L0_ffn_B2").transpose(0, 2, 1).reshape(ER, D))
    w["wg_t"] = np.ascontiguousarray(g("L0_ffn_Wg").T)
    xin = g("L0_attn_a_hidden_states")
    xs = [np.ascontiguousarray(xin[c].T) for c in range(8)]
    return w, xs


def kernel(**inputs):
    mk = MK(dev_outputs=False)
    mk.emit_all()
    nc = mk.finalize()
    w, xs = host_prep(inputs)
    in_maps = [dict(w, x=xs[c]) for c in range(8)]
    res = run_bass_kernel_spmd(nc, in_maps, core_ids=list(range(8)))
    out = np.stack([np.ascontiguousarray(np.asarray(res.results[c]["y"]).T)
                    for c in range(8)])
    return out.astype(np.float32)
